# version 08
# baseline (speedup 1.0000x reference)
import numpy as np
import ml_dtypes
from contextlib import ExitStack
import concourse.bass as bass
import concourse.mybir as mybir
from concourse.bass_utils import run_bass_kernel_spmd

F32 = mybir.dt.float32
BF16 = mybir.dt.bfloat16
AF = mybir.ActivationFunctionType
ALU = mybir.AluOpType
AX = mybir.AxisListType.X

D = 2048
SEQ = 4096
NT = 512
NTILES = SEQ // NT
KC = D // 128
TC = NT // 128
I = 1408
IC = I // 128
E = 8
NH = 16
LN_EPS = 1e-5
ATT_EPS = 1e-6
NCORES = 8
ENGINES = ("sync", "gpsimd", "vector", "scalar", "tensor")
RING_SLOTS = 4
SLOT_COLS = 2048


class Prog:
    def __init__(self):
        self.ops = {e: [] for e in ENGINES}
        self.cnt = {}
        self.waited = {}
        self.sem_names = []

    def sem(self, name):
        if name not in self.cnt:
            self.cnt[name] = 0
            self.sem_names.append(name)
        return name

    def wait(self, eng, sem, val):
        if val is None or val <= 0:
            return
        assert val <= self.cnt[sem], f"wait on future milestone {sem}={val}>{self.cnt[sem]}"
        key = (eng, sem)
        if self.waited.get(key, 0) >= val:
            return
        self.waited[key] = val
        self.ops[eng].append(("w", sem, val))

    def wait_ms(self, eng, ms):
        if ms is not None:
            self.wait(eng, ms[0], ms[1])

    def emit(self, eng, fn, inc=None, amt=1):
        own = getattr(self, "auto", {}).get(eng)
        if own is not None and (inc is None or inc not in self.dma_sems):
            self.wait(eng, own, self.cnt[own])
            if inc is None:
                inc = own
            assert inc == own, f"{eng} op incs {inc} != own {own}"
        m = None
        if inc is not None:
            self.cnt[inc] += amt
            m = self.cnt[inc]
        self.ops[eng].append(("o", fn, inc, amt))
        return m

    def total(self, sem):
        return self.cnt[sem]


class Res:
    __slots__ = ("free",)
    def __init__(self):
        self.free = None


def _col_blocks(W):
    Dout, Din = W.shape
    C, K = Dout // 128, Din // 128
    WT = np.ascontiguousarray(W.T)
    b = WT.reshape(K, 128, C, 128).transpose(2, 1, 0, 3)
    return np.ascontiguousarray(b.reshape(C, 128, K * 128))


def _slab_pieces(W):
    WT = np.ascontiguousarray(W.T)
    b = WT.reshape(4, 4, 128, 4, 512)
    b = b.transpose(3, 0, 2, 1, 4)
    return np.ascontiguousarray(b.reshape(4, 4, 128, 2048))


def _prep_inputs(inp):
    d = {}
    d["sg_blk"] = _col_blocks(inp["L0_ffn_shared_gate"])
    d["su_blk"] = _col_blocks(inp["L0_ffn_shared_up"])
    d["sd_blk"] = _col_blocks(inp["L0_ffn_shared_down"])
    d["wg_blk"] = np.stack([_col_blocks(inp["L0_ffn_w_gate"][e]) for e in range(E)])
    d["wu_blk"] = np.stack([_col_blocks(inp["L0_ffn_w_up"][e]) for e in range(E)])
    d["wd_blk"] = np.stack([_col_blocks(inp["L0_ffn_w_down"][e]) for e in range(E)])
    d["wk_slab"] = _slab_pieces(inp["L0_attn_Wk"])
    d["wv_slab"] = _slab_pieces(inp["L0_attn_Wv"])
    d["wq_blk"] = _col_blocks(inp["L0_attn_Wq"])
    d["wo_blk"] = _col_blocks(inp["L0_attn_Wo"])
    rwT = np.ascontiguousarray(np.asarray(inp["L0_ffn_router_w"]).T)
    d["rw_f32"] = np.ascontiguousarray(rwT.reshape(16, 128, 8).transpose(1, 0, 2).reshape(128, 128))
    d["bq_pp"] = np.ascontiguousarray(np.asarray(inp["L0_attn_bq"]).reshape(16, 128).T)
    d["bo_pp"] = np.ascontiguousarray(np.asarray(inp["L0_attn_bo"]).reshape(16, 128).T)
    d["bk_row"] = np.ascontiguousarray(np.asarray(inp["L0_attn_bk"]).reshape(1, D))
    d["bv_row"] = np.ascontiguousarray(np.asarray(inp["L0_attn_bv"]).reshape(1, D))
    return {k: np.ascontiguousarray(np.asarray(v, dtype=np.float32)) for k, v in d.items()}


def build_nc(ph1_tiles=NTILES, ph2_tiles=NTILES, debug_stop=None):
    nc = bass.Bass("TRN2", target_bir_lowering=False)
    P = Prog()

    x_fm = nc.dram_tensor("x_fm", [D, SEQ], F32, kind="ExternalInput").ap()
    out_fm = nc.dram_tensor("out_fm", [D, SEQ], F32, kind="ExternalOutput").ap()
    x1_fm = nc.dram_tensor("x1_fm", [D, SEQ], F32, kind="Internal").ap()
    dbg_b = nc.dram_tensor("dbg_b", [128, KC * NT], BF16, kind="ExternalOutput").ap()

    w_in, w_b16 = {}, {}
    def wdecl(name, shape):
        w_in[name] = nc.dram_tensor(name, list(shape), F32, kind="ExternalInput").ap()
        w_b16[name] = nc.dram_tensor(name + "_b", list(shape), BF16, kind="Internal").ap()
    wdecl("sg_blk", (IC, 128, KC * 128))
    wdecl("su_blk", (IC, 128, KC * 128))
    wdecl("sd_blk", (KC, 128, IC * 128))
    wdecl("wg_blk", (E, IC, 128, KC * 128))
    wdecl("wu_blk", (E, IC, 128, KC * 128))
    wdecl("wd_blk", (E, KC, 128, IC * 128))
    wdecl("wk_slab", (4, 4, 128, 2048))
    wdecl("wv_slab", (4, 4, 128, 2048))
    wdecl("wq_blk", (KC, 128, KC * 128))
    wdecl("wo_blk", (KC, 128, KC * 128))
    rw_hbm = nc.dram_tensor("rw_f32", [128, 128], F32, kind="ExternalInput").ap()
    bq_hbm = nc.dram_tensor("bq_pp", [128, KC], F32, kind="ExternalInput").ap()
    bo_hbm = nc.dram_tensor("bo_pp", [128, KC], F32, kind="ExternalInput").ap()
    bk_hbm = nc.dram_tensor("bk_row", [1, D], F32, kind="ExternalInput").ap()
    bv_hbm = nc.dram_tensor("bv_row", [1, D], F32, kind="ExternalInput").ap()

    ctx = ExitStack()
    SB = lambda name, shape, dt: ctx.enter_context(nc.sbuf_tensor(name, shape, dt))
    PSA = lambda name, shape: ctx.enter_context(nc.psum_tensor(name, shape, F32))

    stream = SB("stream", [128, KC * NT], F32)
    lnb = SB("lnb", [128, KC * NT], BF16)
    he = SB("he", [128, IC * NT], BF16)
    sig = SB("sig", [128, 2 * NT], BF16)
    uesc = SB("uesc", [128, 2 * NT], BF16)
    v_scr = SB("v_scr", [128, 2 * NT], F32)
    sq_scr = SB("sq_scr", [128, 2 * NT], BF16)
    qtmp = SB("qtmp", [128, 2 * NT], F32)
    z_scr = SB("z_scr", [128, 2 * NT], F32)
    ktm = SB("ktm", [128, TC * D], BF16)
    vtm = SB("vtm", [128, TC * NH * 129], BF16)
    kv_acc = SB("kv_acc", [128, NH * 129], F32)
    kv_b16 = SB("kv_b16", [128, NH * 129], BF16)
    ksrep = SB("ksrep", [128, NH * 128], BF16)
    comb_tm = SB("comb_tm", [128, TC * 8], BF16)
    combT = SB("combT", [8, NT], BF16)
    comb_bc = SB("comb_bc", [128, E * NT], BF16)
    ring = SB("ring", [128, RING_SLOTS * SLOT_COLS], BF16)
    rw_sb = SB("rw_sb", [128, 128], F32)
    wr_row = SB("wr_row", [1, 8], F32)
    wrsum = SB("wrsum", [128, 8], F32)
    bq_sb = SB("bq_sb", [128, KC], F32)
    bo_sb = SB("bo_sb", [128, KC], F32)
    bk_sb = SB("bk_sb", [1, D], F32)
    bv_sb = SB("bv_sb", [1, D], F32)
    mcols = SB("mcols", [128, TC], F32)
    nmcols = SB("nmcols", [128, TC], F32)
    rcols = SB("rcols", [128, TC], F32)
    m2save = SB("m2save", [128, TC * NTILES], F32)
    r2save = SB("r2save", [128, TC * NTILES], F32)
    rep_sb = SB("rep_sb", [128, 2 * 128], F32)
    lrow = SB("lrow", [128, 8], F32)
    lcol = SB("lcol", [128, 8], F32)
    mask8 = SB("mask8", [128, 8], F32)
    em8 = SB("em8", [128, 8], F32)
    decay_cols = SB("decay_cols", [128, SEQ // 128], F32)
    decay_bc = SB("decay_bc", [128, SEQ], BF16)
    rep_b = SB("rep_b", [128, 128], BF16)
    ones_col = SB("ones_col", [128, 1], F32)
    ones_r1f = SB("ones_r1f", [1, 128], F32)
    ones_r1b = SB("ones_r1b", [1, 128], BF16)
    ones128f = SB("ones128f", [128, 128], F32)
    ones128b = SB("ones128b", [128, 128], BF16)
    ones_r512b = SB("ones_r512b", [1, NT], BF16)
    eps_r1b = SB("eps_r1b", [1, 128], BF16)
    idf32 = SB("idf32", [128, 128], F32)
    idb16 = SB("idb16", [128, 128], BF16)
    iota_t = SB("iota_t", [128, 128], F32)
    pcol = SB("pcol", [128, 1], F32)
    epsln_col = SB("epsln_col", [128, 1], F32)
    one1 = SB("one1", [1, 1], F32)
    ones_col_b = SB("ones_col_b", [128, 1], BF16)
    mrow = SB("mrow", [1, NT], F32)
    mean_sb = SB("mean_sb", [128, NT], F32)
    rstd_sb = SB("rstd_sb", [128, NT], F32)
    wrow = SB("wrow", [1, NT], F32)
    bias_bck = SB("bias_bck", [128, D], BF16)
    bias_bcv = SB("bias_bcv", [128, D], BF16)
    iota8 = SB("iota8", [8, E * 128], BF16)
    sel_all = SB("sel_all", [8, E * 128], BF16)

    ps = [PSA(f"ps{i}", [128, NT]) for i in range(4)]
    pkv = [PSA(f"pkv{i}", [128, NT]) for i in range(2)]
    pbc = [PSA(f"pbc{i}", [128, NT]) for i in range(2)]
    ps_r = [Res() for _ in range(4)]
    pkv_r = [Res() for _ in range(2)]
    pbc_r = [Res() for _ in range(2)]
    sig_r = [Res(), Res()]
    sq_r = [Res(), Res()]
    qtmp_r = [Res(), Res()]
    v_r = [Res(), Res()]
    rep_r = [Res(), Res()]
    ktm_r = Res()
    vtm_r = Res()
    lnb_r = Res()
    mean_sb_r = Res()

    s_pe = P.sem("pe"); s_dve = P.sem("dve"); s_act = P.sem("act"); s_pool = P.sem("pool")
    s_ring = [P.sem(f"ring{i}") for i in range(RING_SLOTS)]
    s_dx = P.sem("dx")
    s_dx1st = P.sem("dx1st")
    s_dout = P.sem("dout")
    s_dcast = P.sem("dcast")
    s_dxg = [P.sem(f"dxg{g}") for g in range(4)]
    P.auto = {"vector": s_dve, "scalar": s_act, "gpsimd": s_pool}
    P.dma_sems = set([s_dx, s_dx1st, s_dout, s_dcast] + s_ring + s_dxg)

    def acq(eng, res):
        if res.free is not None:
            P.wait(eng, res.free[0], res.free[1])

    ring_ns = {"next": 0, "free": [0] * RING_SLOTS}

    def ring_load(hbm_ap, cast_key):
        s = ring_ns["next"]
        ring_ns["next"] = (s + 1) % RING_SLOTS
        P.wait("sync", s_pe, ring_ns["free"][s])
        P.wait_ms("sync", cast_ms[cast_key])
        ncols = hbm_ap.shape[-1]
        slot_ap = ring[:, s * SLOT_COLS: s * SLOT_COLS + ncols]
        lv = P.emit("sync", lambda e, o=slot_ap, i=hbm_ap: e.dma_start(out=o, in_=i),
                    inc=s_ring[s], amt=16)
        return s, lv

    def ring_free(slot, pe_ms):
        ring_ns["free"][slot] = pe_ms

    for dst, src in ((rw_sb, rw_hbm), (bq_sb, bq_hbm), (bo_sb, bo_hbm),
                     (bk_sb, bk_hbm), (bv_sb, bv_hbm)):
        P.emit("scalar", lambda e, d=dst, s=src: e.dma_start(out=d[:], in_=s),
               inc=s_dx, amt=16)
    consts_ld = P.total(s_dx)
    for eng in ("tensor", "vector", "scalar"):
        P.wait(eng, s_dx, consts_ld)

    P.emit("gpsimd", lambda e: e.iota(iota_t[:], [[1, 128]], channel_multiplier=0,
                                      allow_small_or_imprecise_dtypes=True))
    P.emit("gpsimd", lambda e: e.iota(pcol[:], [[0, 1]], channel_multiplier=1,
                                      allow_small_or_imprecise_dtypes=True))
    P.emit("gpsimd", lambda e: e.iota(decay_cols[:], [[128, SEQ // 128]], channel_multiplier=1,
                                      allow_small_or_imprecise_dtypes=True))
    P.emit("gpsimd", lambda e: e.memset(ones_col[:], 1.0))
    P.emit("gpsimd", lambda e: e.memset(ones_r1f[:], 1.0))
    P.emit("gpsimd", lambda e: e.memset(ones_r1b[:], 1.0))
    P.emit("gpsimd", lambda e: e.memset(ones128f[:], 1.0))
    P.emit("gpsimd", lambda e: e.memset(ones128b[:], 1.0))
    P.emit("gpsimd", lambda e: e.memset(ones_r512b[:], 1.0))
    P.emit("gpsimd", lambda e: e.memset(eps_r1b[:], ATT_EPS))
    P.emit("gpsimd", lambda e: e.memset(kv_acc[:], 0.0))
    P.emit("gpsimd", lambda e: e.memset(epsln_col[:], LN_EPS))
    P.emit("gpsimd", lambda e: e.memset(one1[:], 1.0))
    P.emit("gpsimd", lambda e: e.memset(ones_col_b[:], 1.0))
    P.emit("gpsimd", lambda e: e.iota(iota8[:], [[1, E], [0, 128]], channel_multiplier=0,
                                      allow_small_or_imprecise_dtypes=True))
    for c in range(TC):
        for h in range(NH):
            off = c * NH * 129 + h * 129 + 128
            P.emit("gpsimd", lambda e, off=off: e.memset(vtm[:, off:off + 1], 1.0))
    pool_init = P.emit("gpsimd", lambda e: e.memset(em8[:], 0.0), inc=s_pool)

    P.wait("vector", s_pool, pool_init)
    P.emit("vector", lambda e: e.tensor_scalar(idf32[:], iota_t[:], pcol[:], None, ALU.is_equal))
    P.emit("vector", lambda e: e.tensor_scalar(idb16[:], iota_t[:], pcol[:], None, ALU.is_equal))
    id_ms = P.emit("vector", lambda e: e.tensor_scalar(sel_all[:], iota8[:], pcol[0:8, :], None, ALU.is_equal),
                   inc=s_dve)
    P.wait("scalar", s_pool, pool_init)
    dec_ms = P.emit("scalar", lambda e: e.activation(decay_cols[:], decay_cols[:], AF.Exp,
                                                     scale=-1.0 / SEQ), inc=s_act)
    P.wait("vector", s_act, dec_ms)
    P.wait("tensor", s_dve, id_ms)
    rep_b_r = Res()
    for gc in range(SEQ // 128):
        pb = pbc[gc % 2]
        acq("vector", rep_b_r)
        dv = P.emit("vector", lambda e, gc=gc: e.tensor_scalar(
            rep_b[:], ones128b[:], decay_cols[:, gc:gc + 1], None, ALU.mult), inc=s_dve)
        P.wait("tensor", s_dve, dv)
        acq("tensor", pbc_r[gc % 2])
        pm = P.emit("tensor", lambda e, pb=pb: e.matmul(
            pb[:, 0:128], rep_b[:], idb16[:], start=True, stop=True), inc=s_pe)
        rep_b_r.free = (s_pe, pm)
        P.wait("vector", s_pe, pm)
        dvc = P.emit("vector", lambda e, gc=gc, pb=pb: e.tensor_copy(
            out=decay_bc[:, gc * 128:(gc + 1) * 128], in_=pb[:, 0:128]), inc=s_dve)
        (pbc_r[gc % 2]).free = (s_dve, dvc)

    P.wait("tensor", s_dx, consts_ld)
    acq("tensor", pkv_r[0])
    for k in range(KC):
        mmi = P.emit("tensor", lambda e, k=k: e.matmul(
            pkv[0][0:1, 0:8], ones_col[:], rw_sb[:, k * 8:(k + 1) * 8],
            start=(k == 0), stop=(k == KC - 1)), inc=(s_pe if k == KC - 1 else None))
    P.wait("vector", s_pe, mmi)
    wrr_ms = P.emit("vector", lambda e: e.tensor_copy(out=wr_row[:], in_=pkv[0][0:1, 0:8]),
                    inc=s_dve)
    pkv_r[0].free = (s_dve, wrr_ms)
    P.wait("tensor", s_dve, wrr_ms)
    acq("tensor", pkv_r[1])
    pm = P.emit("tensor", lambda e: e.matmul(pkv[1][:, 0:8], ones_r1f[:], wr_row[:],
                                             start=True, stop=True), inc=s_pe)
    P.wait("vector", s_pe, pm)
    wrs_ms = P.emit("vector", lambda e: e.tensor_copy(out=wrsum[:], in_=pkv[1][:, 0:8]),
                    inc=s_dve)
    pkv_r[1].free = (s_dve, wrs_ms)

    for brow_sb, bdst in ((bk_sb, bias_bck), (bv_sb, bias_bcv)):
        for seg in range(4):
            pb = pbc[seg % 2]
            acq("tensor", pbc_r[seg % 2])
            pm = P.emit("tensor", lambda e, pb=pb, brow_sb=brow_sb, seg=seg: e.matmul(
                pb[:], ones_r1f[:], brow_sb[:, seg * NT:(seg + 1) * NT],
                start=True, stop=True), inc=s_pe)
            P.wait("vector", s_pe, pm)
            dv = P.emit("vector", lambda e, pb=pb, bdst=bdst, seg=seg: e.tensor_copy(
                out=bdst[:, seg * NT:(seg + 1) * NT], in_=pb[:]), inc=s_dve)
            pbc_r[seg % 2].free = (s_dve, dv)

    cast_ms = {}
    cast_order = [("sg_blk",), ("su_blk",), ("sd_blk",)]
    for e_ in range(E):
        cast_order += [("wg_blk", e_), ("wu_blk", e_), ("wd_blk", e_)]
    cast_order += [("wk_slab",), ("wv_slab",), ("wq_blk",), ("wo_blk",)]
    for ci, key in enumerate(cast_order):
        name = key[0]
        src = w_in[name] if len(key) == 1 else w_in[name][key[1]]
        dst = w_b16[name] if len(key) == 1 else w_b16[name][key[1]]
        csem = P.sem(f"cast{ci}")
        P.dma_sems.add(csem)
        P.emit("gpsimd", lambda e, s=src, d=dst: e.dma_start(out=d, in_=s),
               inc=csem, amt=16)
        cast_ms[key] = (csem, 16)

    rot = {"i": 0}
    lnb_ms = {}
    he_ms = {}
    stream_ms = {}
    kv_ev = {}
    q_ms = {}
    att_ms = {}

    def next_ps(eng="tensor"):
        i = rot["i"] % 4
        rot["i"] += 1
        acq(eng, ps_r[i])
        return ps[i], ps_r[i]

    def wait_lnb(eng):
        for k in range(KC):
            P.wait_ms(eng, lnb_ms[k])

    def ln_stats(t, save_stats, need_cols=True, chunk_ready=None):
        pstat, pstat_r = ps[0], ps_r[0]
        pstatq, pstatq_r = ps[1], ps_r[1]
        acq("tensor", pstat_r)
        acq("tensor", pstatq_r)
        acq("gpsimd", lnb_r)
        for k in range(KC):
            sl = k % 2
            if chunk_ready is not None:
                P.wait_ms("gpsimd", chunk_ready[k])
                P.wait_ms("scalar", chunk_ready[k])
            if k in lnb_ms:
                P.wait_ms("gpsimd", lnb_ms[k])
            xb_ms = P.emit("gpsimd", lambda e, k=k: e.tensor_copy(
                out=lnb[:, k * NT:(k + 1) * NT], in_=stream[:, k * NT:(k + 1) * NT]))
            acq("scalar", sq_r[sl])
            sq_ms = P.emit("scalar", lambda e, k=k, sl=sl: e.activation(
                sq_scr[:, sl * NT:(sl + 1) * NT], stream[:, k * NT:(k + 1) * NT], AF.Square),
                inc=s_act)
            P.wait("tensor", s_pool, xb_ms)
            P.wait("tensor", s_act, sq_ms)
            P.emit("tensor", lambda e, k=k: e.matmul(
                pstat[0:1, 0:NT], ones_col_b[:], lnb[:, k * NT:(k + 1) * NT],
                start=(k == 0), stop=(k == KC - 1)))
            pe_ms = P.emit("tensor", lambda e, k=k, sl=sl: e.matmul(
                pstatq[0:1, 0:NT], ones_col_b[:], sq_scr[:, sl * NT:(sl + 1) * NT],
                start=(k == 0), stop=(k == KC - 1)), inc=s_pe)
            sq_r[sl].free = (s_pe, pe_ms)
        stats_pe = pe_ms
        P.wait("vector", s_pe, stats_pe)
        P.emit("vector", lambda e: e.tensor_scalar(mrow[:], pstat[0:1, 0:NT], 1.0 / D, None, ALU.mult))
        wr_ms0 = P.emit("vector", lambda e: e.tensor_scalar(wrow[:], pstatq[0:1, 0:NT], 1.0 / D, None, ALU.mult),
                        inc=s_dve)
        P.wait("scalar", s_dve, wr_ms0)
        m2_ms = P.emit("scalar", lambda e: e.activation(z_scr[0:1, 0:NT], mrow[:], AF.Square),
                       inc=s_act)
        P.wait("vector", s_act, m2_ms)
        vr_ms = P.emit("vector", lambda e: e.tensor_tensor(wrow[:], wrow[:], z_scr[0:1, 0:NT], ALU.subtract),
                       inc=s_dve)
        P.wait("scalar", s_dve, vr_ms)
        P.emit("scalar", lambda e: e.activation(wrow[:], wrow[:], AF.Ln, bias=epsln_col[0:1, :]))
        rstd_ms = P.emit("scalar", lambda e: e.activation(wrow[:], wrow[:], AF.Exp, scale=-0.5),
                         inc=s_act)
        if need_cols or save_stats:
            P.wait("tensor", s_dve, vr_ms)
            P.wait("tensor", s_act, rstd_ms)
            for c in range(TC):
                P.emit("tensor", lambda e, c=c, pstat=pstat: e.transpose(
                    pstat[:, 64 + 4 * c: 65 + 4 * c], mrow[0:1, c * 128:(c + 1) * 128], one1[:]))
                pe_ms = P.emit("tensor", lambda e, c=c, pstat=pstat: e.transpose(
                    pstat[:, 80 + 4 * c: 81 + 4 * c], wrow[0:1, c * 128:(c + 1) * 128], one1[:]),
                    inc=(s_pe if c == TC - 1 else None))
            P.wait("vector", s_pe, pe_ms)
            for c in range(TC):
                P.emit("vector", lambda e, c=c, pstat=pstat: e.tensor_copy(
                    out=mcols[:, c:c + 1], in_=pstat[:, 64 + 4 * c: 65 + 4 * c]))
                P.emit("vector", lambda e, c=c, pstat=pstat: e.tensor_copy(
                    out=rcols[:, c:c + 1], in_=pstat[:, 80 + 4 * c: 81 + 4 * c]))
            nm_ms = P.emit("vector", lambda e: e.tensor_scalar(nmcols[:], mcols[:], -1.0, None, ALU.mult),
                           inc=s_dve)
            pstat_r.free = (s_dve, nm_ms)
            pstatq_r.free = (s_dve, wr_ms0)
            if save_stats:
                P.emit("vector", lambda e, t=t: e.tensor_copy(out=m2save[:, t * TC:(t + 1) * TC], in_=mcols[:]))
                P.emit("vector", lambda e, t=t: e.tensor_copy(out=r2save[:, t * TC:(t + 1) * TC], in_=rcols[:]))
        else:
            pstat_r.free = (s_dve, vr_ms)
            pstatq_r.free = (s_dve, wr_ms0)
        return rstd_ms

    def ln_finish(with_residual, ready_ms, chunk_ready=None):
        P.wait_ms("tensor", ready_ms)
        acq("tensor", pbc_r[0])
        P.emit("tensor", lambda e: e.matmul(pbc[0][:], ones_r1f[:], mrow[:],
                                            start=True, stop=True))
        acq("tensor", pbc_r[1])
        bc_ms = P.emit("tensor", lambda e: e.matmul(pbc[1][:], ones_r1f[:], wrow[:],
                                                    start=True, stop=True), inc=s_pe)
        P.wait("vector", s_pe, bc_ms)
        acq("vector", mean_sb_r)
        P.emit("vector", lambda e: e.tensor_copy(out=mean_sb[:], in_=pbc[0][:]))
        sbc_ms = P.emit("vector", lambda e: e.tensor_copy(out=rstd_sb[:], in_=pbc[1][:]),
                        inc=s_dve)
        return normalize(with_residual, bc_ms, sbc_ms, chunk_ready=chunk_ready)

    def broadcast_cols(ready_ms):
        P.wait_ms("vector", ready_ms)
        for c in range(TC):
            sl = c % 2
            acq("vector", rep_r[sl])
            dv = P.emit("vector", lambda e, c=c, sl=sl: e.tensor_scalar(
                rep_sb[:, sl * 128:(sl + 1) * 128], ones128f[:], mcols[:, c:c + 1], None, ALU.mult),
                inc=s_dve)
            P.wait("tensor", s_dve, dv)
            if c == 0:
                acq("tensor", pbc_r[0])
            pm = P.emit("tensor", lambda e, c=c, sl=sl: e.matmul(
                pbc[0][:, c * 128:(c + 1) * 128], rep_sb[:, sl * 128:(sl + 1) * 128], idf32[:],
                start=True, stop=True), inc=s_pe)
            rep_r[sl].free = (s_pe, pm)
            acq("vector", rep_r[sl])
            dv = P.emit("vector", lambda e, c=c, sl=sl: e.tensor_scalar(
                rep_sb[:, sl * 128:(sl + 1) * 128], ones128f[:], rcols[:, c:c + 1], None, ALU.mult),
                inc=s_dve)
            P.wait("tensor", s_dve, dv)
            if c == 0:
                acq("tensor", pbc_r[1])
            pm = P.emit("tensor", lambda e, c=c, sl=sl: e.matmul(
                pbc[1][:, c * 128:(c + 1) * 128], rep_sb[:, sl * 128:(sl + 1) * 128], idf32[:],
                start=True, stop=True), inc=s_pe)
            rep_r[sl].free = (s_pe, pm)
        return pm

    def normalize(with_residual, bc_ms, sbc_ms, chunk_ready=None):
        P.wait("vector", s_pe, bc_ms)
        P.wait("gpsimd", s_pe, bc_ms)
        last_dve = None
        last_pool = None
        for k in range(KC):
            on_dve = (k % 2 == 0)
            sl = 0 if on_dve else 1
            eng = "vector" if on_dve else "gpsimd"
            if chunk_ready is not None:
                P.wait_ms(eng, chunk_ready[k])
            if on_dve:
                P.emit("vector", lambda e, k=k, sl=sl: e.tensor_tensor(
                    v_scr[:, sl * NT:(sl + 1) * NT], stream[:, k * NT:(k + 1) * NT],
                    pbc[0][:], ALU.subtract))
                lm = P.emit("vector", lambda e, k=k, sl=sl: e.tensor_tensor(
                    lnb[:, k * NT:(k + 1) * NT], v_scr[:, sl * NT:(sl + 1) * NT],
                    pbc[1][:], ALU.mult), inc=s_dve)
                last_dve = lm
                lnb_ms[k] = (s_dve, lm)
            else:
                P.wait_ms("gpsimd", (s_dve, sbc_ms))
                P.emit("gpsimd", lambda e, k=k, sl=sl: e.tensor_tensor(
                    v_scr[:, sl * NT:(sl + 1) * NT], stream[:, k * NT:(k + 1) * NT],
                    mean_sb[:], ALU.subtract))
                lm = P.emit("gpsimd", lambda e, k=k, sl=sl: e.tensor_tensor(
                    lnb[:, k * NT:(k + 1) * NT], v_scr[:, sl * NT:(sl + 1) * NT],
                    rstd_sb[:], ALU.mult), inc=s_pool)
                last_pool = lm
                lnb_ms[k] = (s_pool, lm)
        pbc_r[0].free = (s_dve, last_dve)
        pbc_r[1].free = (s_dve, last_dve)
        if last_pool is not None:
            mean_sb_r.free = (s_pool, last_pool)
        return last_dve

    def residual_update(gate_ms=None):
        last = None
        for k in range(KC):
            P.wait_ms("vector", lnb_ms[k])
            if gate_ms is not None:
                P.wait_ms("vector", gate_ms)
            last = P.emit("vector", lambda e, k=k: e.scalar_tensor_tensor(
                stream[:, k * NT:(k + 1) * NT], lnb[:, k * NT:(k + 1) * NT], 0.5,
                stream[:, k * NT:(k + 1) * NT], ALU.mult, ALU.add), inc=s_dve)
        return last

    def router_logits(t):
        acq("tensor", pkv_r[1])
        gm = None
        for c in range(TC):
            for k in range(KC):
                gm = P.emit("tensor", lambda e, c=c, k=k: e.matmul(
                    pkv[1][:, c * 8:(c + 1) * 8],
                    stream[:, k * NT + c * 128: k * NT + (c + 1) * 128],
                    rw_sb[:, k * 8:(k + 1) * 8],
                    start=(k == 0), stop=(k == KC - 1)),
                    inc=(s_pe if (k == KC - 1 and c == TC - 1) else None))
        return gm

    def router_comb(t, gm, nmc, rc):
        ctv = None
        for c in range(TC):
            P.wait("vector", s_pe, gm)
            P.wait("vector", s_dve, wrs_ms)
            P.emit("vector", lambda e, c=c, nmc=nmc: e.scalar_tensor_tensor(
                lrow[:], wrsum[:], nmc[:, c:c + 1], pkv[1][:, c * 8:(c + 1) * 8],
                ALU.mult, ALU.add))
            P.emit("vector", lambda e: e.tensor_reduce(
                lcol[:, 0:1], lrow[:], axis=AX, op=ALU.max, negate=True))
            hm = P.emit("vector", lambda e, c=c, rc=rc: e.tensor_scalar(
                lrow[:], lrow[:], lcol[:, 0:1], rc[:, c:c + 1], ALU.add, ALU.mult),
                inc=s_dve)
            P.wait("scalar", s_dve, hm)
            em = P.emit("scalar", lambda e: e.activation(lrow[:], lrow[:], AF.Exp), inc=s_act)
            P.wait("vector", s_act, em)
            P.emit("vector", lambda e: e.tensor_reduce(lcol[:, 1:2], lrow[:], axis=AX, op=ALU.add))
            P.emit("vector", lambda e: e.reciprocal(lcol[:, 2:3], lcol[:, 1:2]))
            P.emit("vector", lambda e: e.tensor_reduce(lcol[:, 3:4], lrow[:], axis=AX, op=ALU.max))
            P.emit("vector", lambda e: e.tensor_scalar(mask8[:], lrow[:], lcol[:, 3:4], None, ALU.is_ge))
            P.emit("vector", lambda e: e.scalar_tensor_tensor(em8[:], mask8[:], -1e30, lrow[:],
                                                              ALU.mult, ALU.add))
            P.emit("vector", lambda e: e.tensor_reduce(lcol[:, 5:6], em8[:], axis=AX, op=ALU.max))
            P.emit("vector", lambda e: e.tensor_scalar(mask8[:], lrow[:], lcol[:, 5:6], None, ALU.is_ge))
            cm = P.emit("vector", lambda e, c=c: e.scalar_tensor_tensor(
                comb_tm[:, c * 8:(c + 1) * 8], mask8[:], lcol[:, 2:3], lrow[:],
                ALU.mult, ALU.mult), inc=s_dve)
            if c == TC - 1:
                pkv_r[1].free = (s_dve, cm)
            P.wait("tensor", s_dve, cm)
            if c == 0:
                acq("tensor", pkv_r[0])
            pm = P.emit("tensor", lambda e, c=c: e.matmul(
                pkv[0][0:8, c * 128:(c + 1) * 128], comb_tm[:, c * 8:(c + 1) * 8], idb16[:],
                start=True, stop=True), inc=s_pe)
            P.wait("vector", s_pe, pm)
            ctv = P.emit("vector", lambda e, c=c: e.tensor_copy(
                out=combT[:, c * 128:(c + 1) * 128], in_=pkv[0][0:8, c * 128:(c + 1) * 128]),
                inc=s_dve)
        pkv_r[0].free = (s_dve, ctv)
        P.wait("tensor", s_dve, ctv)
        last = None
        for ee in range(E):
            pb, pr = pbc[ee % 2], pbc_r[ee % 2]
            acq("tensor", pr)
            pm = P.emit("tensor", lambda e, ee=ee, pb=pb: e.matmul(
                pb[:], sel_all[:, ee * 128:(ee + 1) * 128], combT[:], start=True, stop=True), inc=s_pe)
            P.wait("vector", s_pe, pm)
            last = P.emit("vector", lambda e, ee=ee, pb=pb: e.tensor_copy(
                out=comb_bc[:, ee * NT:(ee + 1) * NT], in_=pb[:]), inc=s_dve)
            pr.free = (s_dve, last)
        return last

    def ffn_stage(t, comb_ready, experts=None):
        last_sm = None
        if experts is None:
            experts = [None] + list(range(E))
        for ex in experts:
            is_shared = ex is None
            for c in range(IC):
                gsrc = w_b16["sg_blk"][c] if is_shared else w_b16["wg_blk"][ex][c]
                usrc = w_b16["su_blk"][c] if is_shared else w_b16["wu_blk"][ex][c]
                gkey = ("sg_blk",) if is_shared else ("wg_blk", ex)
                ukey = ("su_blk",) if is_shared else ("wu_blk", ex)
                pg, pgr = next_ps()
                pu, pur = next_ps()
                for src, key, pp in ((gsrc, gkey, pg), (usrc, ukey, pu)):
                    slot, lv = ring_load(src, key)
                    P.wait("tensor", s_ring[slot], lv)
                    for k in range(KC):
                        P.wait_ms("tensor", lnb_ms[k])
                        mm = P.emit("tensor", lambda e, slot=slot, k=k, pp=pp: e.matmul(
                            pp[:], ring[:, slot * SLOT_COLS + k * 128: slot * SLOT_COLS + (k + 1) * 128],
                            lnb[:, k * NT:(k + 1) * NT], start=(k == 0), stop=(k == KC - 1)),
                            inc=(s_pe if k == KC - 1 else None))
                    ring_free(slot, mm)
                g_pe, u_pe = None, mm
                P.wait("scalar", s_pe, u_pe)
                sl = c % 2
                acq("scalar", sig_r[sl])
                am = P.emit("scalar", lambda e, pg=pg, sl=sl: e.activation(
                    sig[:, sl * NT:(sl + 1) * NT], pg[:], AF.Sigmoid), inc=s_act)
                P.wait("vector", s_act, am)
                if is_shared:
                    hm = P.emit("vector", lambda e, c=c, pu=pu, sl=sl: e.tensor_tensor(
                        he[:, c * NT:(c + 1) * NT], sig[:, sl * NT:(sl + 1) * NT], pu[:],
                        ALU.mult), inc=s_dve)
                else:
                    P.wait_ms("vector", comb_ready)
                    P.emit("vector", lambda e, pu=pu, sl=sl, ex=ex: e.tensor_tensor(
                        uesc[:, sl * NT:(sl + 1) * NT], pu[:],
                        comb_bc[:, ex * NT:(ex + 1) * NT], ALU.mult))
                    hm = P.emit("vector", lambda e, c=c, sl=sl: e.tensor_tensor(
                        he[:, c * NT:(c + 1) * NT], sig[:, sl * NT:(sl + 1) * NT],
                        uesc[:, sl * NT:(sl + 1) * NT], ALU.mult), inc=s_dve)
                sig_r[sl].free = (s_dve, hm)
                pgr.free = (s_dve, hm)
                pur.free = (s_dve, hm)
                he_ms[c] = hm
            for oc in range(KC):
                dsrc = w_b16["sd_blk"][oc] if is_shared else w_b16["wd_blk"][ex][oc]
                dkey = ("sd_blk",) if is_shared else ("wd_blk", ex)
                pd, pdr = next_ps()
                slot, lv = ring_load(dsrc, dkey)
                P.wait("tensor", s_ring[slot], lv)
                for ic in range(IC):
                    P.wait("tensor", s_dve, he_ms[ic])
                    mm = P.emit("tensor", lambda e, slot=slot, ic=ic, pd=pd: e.matmul(
                        pd[:], ring[:, slot * SLOT_COLS + ic * 128: slot * SLOT_COLS + (ic + 1) * 128],
                        he[:, ic * NT:(ic + 1) * NT], start=(ic == 0), stop=(ic == IC - 1)),
                        inc=(s_pe if ic == IC - 1 else None))
                ring_free(slot, mm)
                P.wait("vector", s_pe, mm)
                sm = P.emit("vector", lambda e, oc=oc, pd=pd: e.scalar_tensor_tensor(
                    stream[:, oc * NT:(oc + 1) * NT], pd[:], 0.5,
                    stream[:, oc * NT:(oc + 1) * NT], ALU.mult, ALU.add), inc=s_dve)
                pdr.free = (s_dve, sm)
                stream_ms[oc] = sm
                last_sm = sm
        return last_sm

    def kv_stage(t):
        for mat, bias_row in (("wk_slab", bk_sb), ("wv_slab", bv_sb)):
            for dseg in range(4):
                for c in range(TC):
                    acq("tensor", ps_r[c])
                for kg in range(4):
                    src = w_b16[mat][dseg][kg]
                    slot, lv = ring_load(src, (mat,))
                    P.wait("tensor", s_ring[slot], lv)
                    for c in range(TC):
                        for kk in range(4):
                            k = kg * 4 + kk
                            P.wait_ms("tensor", lnb_ms[k])
                            mm = P.emit("tensor", lambda e, slot=slot, kk=kk, k=k, c=c, kg=kg: e.matmul(
                                ps[c][:], lnb[:, k * NT + c * 128: k * NT + (c + 1) * 128],
                                ring[:, slot * SLOT_COLS + kk * 512: slot * SLOT_COLS + (kk + 1) * 512],
                                start=(kg == 0 and kk == 0), stop=(kg == 3 and kk == 3)),
                                inc=(s_pe if (c == TC - 1 and kk == 3) else None))
                    ring_free(slot, mm)
                for c in range(TC):
                    pe_ms = mm if c == TC - 1 else None
                    P.wait("vector" if mat == "wv_slab" else "vector", s_pe, mm)
                    if mat == "wk_slab":
                        bb = bias_bck
                        sl = c % 2
                        acq("vector", qtmp_r[sl])
                        dv = P.emit("vector", lambda e, c=c, dseg=dseg, sl=sl, bb=bb: e.tensor_tensor(
                            qtmp[:, sl * NT:(sl + 1) * NT], ps[c][:],
                            bb[:, dseg * 512:(dseg + 1) * 512], ALU.add), inc=s_dve)
                        ps_r[c].free = (s_dve, dv)
                        P.wait("scalar", s_dve, dv)
                        if t == 0 and dseg == 0 and c == 0:
                            acq("scalar", ktm_r)
                        ev = P.emit("scalar", lambda e, t=t, c=c, dseg=dseg, sl=sl: e.activation(
                            ktm[:, c * D + dseg * 512: c * D + (dseg + 1) * 512],
                            qtmp[:, sl * NT:(sl + 1) * NT],
                            AF.Relu, scale=decay_cols[:, t * TC + c: t * TC + c + 1]), inc=s_act)
                        qtmp_r[sl].free = (s_act, ev)
                        kv_ev[(mat, dseg, c)] = (s_act, ev)
                    else:
                        if t == 0 and dseg == 0 and c == 0:
                            acq("vector", vtm_r)
                        base = c * NH * 129 + dseg * 4 * 129
                        dst_ap = vtm[:, base: base + 4 * 129].rearrange(
                            "p (h d) -> p h d", h=4)[:, :, 0:128]
                        ev = P.emit("vector", lambda e, c=c, dseg=dseg, dst_ap=dst_ap: e.tensor_tensor(
                            out=dst_ap, in0=ps[c][:].rearrange("p (h d) -> p h d", h=4),
                            in1=bias_bcv[:, dseg * 512:(dseg + 1) * 512].rearrange("p (h d) -> p h d", h=4),
                            op=ALU.add),
                            inc=s_dve)
                        kv_ev[(mat, dseg, c)] = (s_dve, ev)
                        ps_r[c].free = (s_dve, ev)
        lnb_r.free = (s_pe, P.total(s_pe))
        return None

    def kv_accum(t, heads=range(NH)):
        last_dv = None
        for h in heads:
            dseg = h // 4
            for c in range(TC):
                P.wait_ms("tensor", kv_ev[("wk_slab", dseg, c)])
                P.wait_ms("tensor", kv_ev[("wv_slab", dseg, c)])
            pp, pr = pkv[h % 2], pkv_r[h % 2]
            acq("tensor", pr)
            for c in range(TC):
                mm = P.emit("tensor", lambda e, h=h, c=c, pp=pp: e.matmul(
                    pp[:, 0:129], ktm[:, c * D + h * 128: c * D + (h + 1) * 128],
                    vtm[:, c * NH * 129 + h * 129: c * NH * 129 + (h + 1) * 129],
                    start=(c == 0), stop=(c == TC - 1)),
                    inc=(s_pe if c == TC - 1 else None))
            P.wait("vector", s_pe, mm)
            last_dv = P.emit("vector", lambda e, h=h, pp=pp: e.tensor_tensor(
                kv_acc[:, h * 129:(h + 1) * 129], pp[:, 0:129],
                kv_acc[:, h * 129:(h + 1) * 129], ALU.add), inc=s_dve)
            pr.free = (s_dve, last_dv)
        if list(heads)[-1] == NH - 1:
            ktm_r.free = (s_pe, P.total(s_pe))
            vtm_r.free = (s_pe, P.total(s_pe))
            lnb_r.free = (s_pe, P.total(s_pe))
        return last_dv

    x1_store_ms = [None] * NTILES
    for t in range(ph1_tiles):
        if t > 0:
            P.wait("scalar", s_dx1st, x1_store_ms[t - 1])
        xl_gms = {}
        for g in range(4):
            xl = P.emit("scalar", lambda e, t=t, g=g: e.dma_start(
                out=stream[:].rearrange("p (k n) -> p k n", k=KC)[:, 4 * g:4 * g + 4, :],
                in_=x_fm[:, t * NT:(t + 1) * NT].rearrange(
                    "(k p) n -> p k n", p=128)[:, 4 * g:4 * g + 4, :]),
                inc=s_dxg[g], amt=16)
            for k in range(4 * g, 4 * g + 4):
                xl_gms[k] = (s_dxg[g], xl)
        rms = ln_stats(t, save_stats=False, chunk_ready=xl_gms)
        if debug_stop == "stats":
            P.wait("gpsimd", s_act, rms)
            P.emit("gpsimd", lambda e: e.dma_start(out=dbg_b[:, 0:TC], in_=mcols[:]), inc=s_dout, amt=16)
            P.emit("gpsimd", lambda e: e.dma_start(out=dbg_b[:, TC:2 * TC], in_=rcols[:]), inc=s_dout, amt=16)
            P.emit("gpsimd", lambda e: e.dma_start(out=dbg_b[:, 2 * TC:5 * TC], in_=vcols[:]), inc=s_dout, amt=16)
            P.emit("gpsimd", lambda e: e.dma_start(out=dbg_b[:, 5 * TC:6 * TC], in_=nmcols[:]), inc=s_dout, amt=16)
            P.wait("sync", s_dout, P.total(s_dout))
            _finish(nc, ctx, P)
            return nc
        ln_finish(True, (s_act, rms))
        if t > 0:
            kv_accum(t - 1, heads=range(0, 8))
        gm = router_logits(t)
        residual_update((s_pe, P.total(s_pe)))
        if debug_stop in ("ln1", "router"):
            if debug_stop == "router":
                router_comb(t, gm, nmcols, rcols)
            P.wait("scalar", s_dve, P.total(s_dve))
            P.wait("scalar", s_pool, P.total(s_pool))
            src = lnb if debug_stop == "ln1" else comb_bc
            P.emit("scalar", lambda e, src=src: e.dma_start(
                out=dbg_b[:, 0:src.shape[-1]], in_=src[:]), inc=s_dout, amt=16)
            P.emit("scalar", lambda e, t=t: e.dma_start(
                out=out_fm[:, t * NT:(t + 1) * NT].rearrange("(k p) n -> p k n", p=128),
                in_=stream[:].rearrange("p (k n) -> p k n", k=KC)), inc=s_dout, amt=16)
            P.wait("sync", s_dout, P.total(s_dout))
            _finish(nc, ctx, P)
            return nc
        ffn_stage(t, None, experts=[None])
        if debug_stop != "ffnsh":
            comb_last = router_comb(t, gm, nmcols, rcols)
            ffn_stage(t, (s_dve, comb_last), experts=list(range(E)))
        lnb_r.free = (s_pe, P.total(s_pe))
        if debug_stop in ("ffnsh", "ffn1"):
            for oc in range(KC):
                P.wait("scalar", s_dve, stream_ms[oc])
            P.wait("scalar", s_dve, P.total(s_dve))
            P.emit("scalar", lambda e, t=t: e.dma_start(
                out=out_fm[:, t * NT:(t + 1) * NT].rearrange("(k p) n -> p k n", p=128),
                in_=stream[:].rearrange("p (k n) -> p k n", k=KC)), inc=s_dout, amt=16)
            P.wait("sync", s_dout, P.total(s_dout))
            _finish(nc, ctx, P)
            return nc
        rms = ln_stats(t, save_stats=True, chunk_ready={k: (s_dve, v) for k, v in stream_ms.items()})
        ln_finish(False, (s_act, rms))
        if t > 0:
            kv_accum(t - 1, heads=range(8, NH))
        P.wait("scalar", s_dve, P.total(s_dve))
        P.wait("scalar", s_pool, P.total(s_pool))
        x1_store_ms[t] = P.emit("scalar", lambda e, t=t: e.dma_start(
            out=x1_fm[:, t * NT:(t + 1) * NT].rearrange("(k p) n -> p k n", p=128),
            in_=stream[:].rearrange("p (k n) -> p k n", k=KC)), inc=s_dx1st, amt=16)
        if debug_stop == "ph1":
            P.emit("scalar", lambda e, t=t: e.dma_start(
                out=out_fm[:, t * NT:(t + 1) * NT].rearrange("(k p) n -> p k n", p=128),
                in_=stream[:].rearrange("p (k n) -> p k n", k=KC)), inc=s_dout, amt=16)
        kv_stage(t)
    if ph1_tiles > 0:
        kv_accum(ph1_tiles - 1, heads=range(0, 8))
        kv_accum(ph1_tiles - 1, heads=range(8, NH))

    if debug_stop == "ph1" or ph2_tiles == 0:
        P.wait("sync", s_dout, P.total(s_dout))
        P.wait("sync", s_dx1st, P.total(s_dx1st))
        _finish(nc, ctx, P)
        return nc

    P.emit("vector", lambda e: e.tensor_copy(out=kv_b16[:], in_=kv_acc[:]))
    for h in range(NH):
        kvf = P.emit("vector", lambda e, h=h: e.tensor_scalar(
            ksrep[:, h * 128:(h + 1) * 128], ones128b[:],
            kv_acc[:, h * 129 + 128: h * 129 + 129], None, ALU.mult), inc=s_dve)
    kv_final = kvf

    out_store_ms = [None] * NTILES
    for t in range(ph2_tiles):
        if t == 0:
            P.wait("scalar", s_dx1st, P.total(s_dx1st))
        else:
            P.wait("scalar", s_dout, P.total(s_dout))
        xl_gms = {}
        for g in range(4):
            xl = P.emit("scalar", lambda e, t=t, g=g: e.dma_start(
                out=stream[:].rearrange("p (k n) -> p k n", k=KC)[:, 4 * g:4 * g + 4, :],
                in_=x1_fm[:, t * NT:(t + 1) * NT].rearrange(
                    "(k p) n -> p k n", p=128)[:, 4 * g:4 * g + 4, :]),
                inc=s_dxg[g], amt=16)
            for k in range(4 * g, 4 * g + 4):
                xl_gms[k] = (s_dxg[g], xl)

        P.emit("vector", lambda e, t=t: e.tensor_copy(out=mcols[:], in_=m2save[:, t * TC:(t + 1) * TC]))
        rsv = P.emit("vector", lambda e, t=t: e.tensor_copy(out=rcols[:], in_=r2save[:, t * TC:(t + 1) * TC]),
                     inc=s_dve)
        broadcast_cols((s_dve, rsv))
        bm = P.total(s_pe)
        P.wait("vector", s_pe, bm)
        acq("vector", mean_sb_r)
        P.emit("vector", lambda e: e.tensor_copy(out=mean_sb[:], in_=pbc[0][:]))
        sbc = P.emit("vector", lambda e: e.tensor_copy(out=rstd_sb[:], in_=pbc[1][:]),
                     inc=s_dve)
        normalize(False, bm, sbc, chunk_ready=xl_gms)

        P.wait("vector", s_dve, kv_final)

        def q_block(oc):
            pq, pqr = next_ps()
            slot, lv = ring_load(w_b16["wq_blk"][oc], ("wq_blk",))
            P.wait("tensor", s_ring[slot], lv)
            for k in range(KC):
                P.wait_ms("tensor", lnb_ms[k])
                mm = P.emit("tensor", lambda e, slot=slot, k=k, pq=pq: e.matmul(
                    pq[:], ring[:, slot * SLOT_COLS + k * 128: slot * SLOT_COLS + (k + 1) * 128],
                    lnb[:, k * NT:(k + 1) * NT], start=(k == 0), stop=(k == KC - 1)),
                    inc=(s_pe if k == KC - 1 else None))
            ring_free(slot, mm)
            P.wait("scalar", s_pe, mm)
            sl = oc % 2
            acq("scalar", qtmp_r[sl])
            am = P.emit("scalar", lambda e, pq=pq, sl=sl, oc=oc: e.activation(
                qtmp[:, sl * NT:(sl + 1) * NT], pq[:], AF.Relu, bias=bq_sb[:, oc:oc + 1]),
                inc=s_act)
            P.wait("vector", s_act, am)
            acq("vector", ktm_r)
            qm = P.emit("vector", lambda e, t=t, oc=oc, sl=sl: e.tensor_tensor(
                ktm[:, oc * NT:(oc + 1) * NT], qtmp[:, sl * NT:(sl + 1) * NT],
                decay_bc[:, t * NT:(t + 1) * NT], ALU.mult), inc=s_dve)
            qtmp_r[sl].free = (s_dve, qm)
            pqr.free = (s_dve, qm)
            q_ms[oc] = qm

        def attn_head(h):
            P.wait("tensor", s_dve, q_ms[h])
            pa, par = next_ps()
            pz, pzr = next_ps()
            P.emit("tensor", lambda e, h=h, pa=pa: e.matmul(
                pa[:], kv_b16[:, h * 129: h * 129 + 128], ktm[:, h * NT:(h + 1) * NT],
                start=True, stop=True))
            P.emit("tensor", lambda e, h=h, pz=pz: e.matmul(
                pz[:], ksrep[:, h * 128:(h + 1) * 128], ktm[:, h * NT:(h + 1) * NT],
                start=True, stop=False))
            pe_ms = P.emit("tensor", lambda e, pz=pz: e.matmul(
                pz[:], eps_r1b[:], ones_r512b[:], start=False, stop=True), inc=s_pe)
            P.wait("vector", s_pe, pe_ms)
            sl = h % 2
            P.emit("vector", lambda e, pz=pz, sl=sl: e.reciprocal(
                z_scr[:, sl * NT:(sl + 1) * NT], pz[:]))
            acq("vector", vtm_r)
            av = P.emit("vector", lambda e, h=h, pa=pa, sl=sl: e.tensor_tensor(
                vtm[:, h * NT:(h + 1) * NT], pa[:], z_scr[:, sl * NT:(sl + 1) * NT],
                ALU.mult), inc=s_dve)
            par.free = (s_dve, av)
            pzr.free = (s_dve, av)
            att_ms[h] = av

        for oc in range(KC):
            q_block(oc)
            if oc >= 1:
                attn_head(oc - 1)
        lnb_r.free = (s_pe, P.total(s_pe))
        attn_head(NH - 1)

        for oc in range(KC):
            po, por = next_ps()
            slot, lv = ring_load(w_b16["wo_blk"][oc], ("wo_blk",))
            P.wait("tensor", s_ring[slot], lv)
            for k in range(KC):
                P.wait("tensor", s_dve, att_ms[k])
                mm = P.emit("tensor", lambda e, slot=slot, k=k, po=po: e.matmul(
                    po[:], ring[:, slot * SLOT_COLS + k * 128: slot * SLOT_COLS + (k + 1) * 128],
                    vtm[:, k * NT:(k + 1) * NT], start=(k == 0), stop=(k == KC - 1)),
                    inc=(s_pe if k == KC - 1 else None))
            ring_free(slot, mm)
            P.wait("vector", s_pe, mm)
            sm = P.emit("vector", lambda e, oc=oc, po=po: e.scalar_tensor_tensor(
                stream[:, oc * NT:(oc + 1) * NT], po[:], bo_sb[:, oc:oc + 1],
                stream[:, oc * NT:(oc + 1) * NT], ALU.add, ALU.add), inc=s_dve)
            por.free = (s_dve, sm)
            stream_ms[oc] = sm
        ktm_r.free = (s_pe, P.total(s_pe))
        vtm_r.free = (s_pe, P.total(s_pe))

        rms = ln_stats(t, save_stats=False, chunk_ready={k: (s_dve, v) for k, v in stream_ms.items()})
        ln_finish(True, (s_act, rms))
        gm = router_logits(t)
        residual_update((s_pe, P.total(s_pe)))
        ffn_stage(t, None, experts=[None])
        comb_last = router_comb(t, gm, nmcols, rcols)
        ffn_stage(t, (s_dve, comb_last), experts=list(range(E)))
        lnb_r.free = (s_pe, P.total(s_pe))

        for g in range(4):
            for oc in range(4 * g, 4 * g + 4):
                P.wait("scalar", s_dve, stream_ms[oc])
            out_store_ms[t] = P.emit("scalar", lambda e, t=t, g=g: e.dma_start(
                out=out_fm[:, t * NT:(t + 1) * NT].rearrange(
                    "(k p) n -> p k n", p=128)[:, 4 * g:4 * g + 4, :],
                in_=stream[:].rearrange("p (k n) -> p k n", k=KC)[:, 4 * g:4 * g + 4, :]),
                inc=s_dout, amt=16)

    P.wait("sync", s_dout, P.total(s_dout))
    _finish(nc, ctx, P)
    return nc


def _finish(nc, ctx, P):
    semh = {name: ctx.enter_context(nc.semaphore(name)) for name in P.sem_names}
    with nc.Block() as block:
        def mk_section(eng_name):
            def section(eng):
                for item in P.ops[eng_name]:
                    if item[0] == "w":
                        eng.wait_ge(semh[item[1]], item[2])
                    else:
                        _, fn, inc, amt = item
                        instr = fn(eng)
                        if inc is not None:
                            instr.then_inc(semh[inc], amt)
            return section
        block.sync(mk_section("sync"))
        block.gpsimd(mk_section("gpsimd"))
        block.vector(mk_section("vector"))
        block.scalar(mk_section("scalar"))
        block.tensor(mk_section("tensor"))
    ctx.close()


def kernel(**inputs):
    w = _prep_inputs(inputs)
    x = np.asarray(inputs["L0_attn_x"], dtype=np.float32)
    nc = build_nc()
    in_maps = []
    for core in range(NCORES):
        m = dict(w)
        m["x_fm"] = np.ascontiguousarray(x[core].T)
        in_maps.append(m)
    res = run_bass_kernel_spmd(nc, in_maps, core_ids=list(range(NCORES)))
    out = np.stack([np.asarray(r["out_fm"]).T for r in res.results])
    return np.ascontiguousarray(out)


if __name__ == "__main__":
    import reference as _ref
    inp = _ref.setup_inputs(0)
    inp = {k: np.asarray(v) for k, v in inp.items()}
    got = kernel(**inp)
    want = np.asarray(_ref.reference(**inp))
    num = float(((got - want) ** 2).mean())
    den = float((want ** 2).mean())
    print("resvar", num / den)


# version 09
# speedup vs baseline: 1.0390x; 1.0390x over previous
import numpy as np
import ml_dtypes
from contextlib import ExitStack
import concourse.bass as bass
import concourse.mybir as mybir
from concourse.bass_utils import run_bass_kernel_spmd

F32 = mybir.dt.float32
BF16 = mybir.dt.bfloat16
AF = mybir.ActivationFunctionType
ALU = mybir.AluOpType
AX = mybir.AxisListType.X

D = 2048
SEQ = 4096
NT = 512
NTILES = SEQ // NT
KC = D // 128
TC = NT // 128
I = 1408
IC = I // 128
E = 8
NH = 16
LN_EPS = 1e-5
ATT_EPS = 1e-6
NCORES = 8
ENGINES = ("sync", "gpsimd", "vector", "scalar", "tensor")
RING_SLOTS = 4
SLOT_COLS = 2048


class Prog:
    def __init__(self):
        self.ops = {e: [] for e in ENGINES}
        self.cnt = {}
        self.waited = {}
        self.sem_names = []

    def sem(self, name):
        if name not in self.cnt:
            self.cnt[name] = 0
            self.sem_names.append(name)
        return name

    def wait(self, eng, sem, val):
        if val is None or val <= 0:
            return
        assert val <= self.cnt[sem], f"wait on future milestone {sem}={val}>{self.cnt[sem]}"
        key = (eng, sem)
        if self.waited.get(key, 0) >= val:
            return
        self.waited[key] = val
        self.ops[eng].append(("w", sem, val))

    def wait_ms(self, eng, ms):
        if ms is not None:
            self.wait(eng, ms[0], ms[1])

    def emit(self, eng, fn, inc=None, amt=1):
        own = getattr(self, "auto", {}).get(eng)
        if own is not None and (inc is None or inc not in self.dma_sems):
            self.wait(eng, own, self.cnt[own])
            if inc is None:
                inc = own
            assert inc == own, f"{eng} op incs {inc} != own {own}"
        m = None
        if inc is not None:
            self.cnt[inc] += amt
            m = self.cnt[inc]
        self.ops[eng].append(("o", fn, inc, amt))
        return m

    def total(self, sem):
        return self.cnt[sem]


class Res:
    __slots__ = ("free",)
    def __init__(self):
        self.free = None


def _col_blocks(W):
    Dout, Din = W.shape
    C, K = Dout // 128, Din // 128
    WT = np.ascontiguousarray(W.T)
    b = WT.reshape(K, 128, C, 128).transpose(2, 1, 0, 3)
    return np.ascontiguousarray(b.reshape(C, 128, K * 128))


def _slab_pieces(W):
    WT = np.ascontiguousarray(W.T)
    b = WT.reshape(4, 4, 128, 4, 512)
    b = b.transpose(3, 0, 2, 1, 4)
    return np.ascontiguousarray(b.reshape(4, 4, 128, 2048))


def _prep_inputs(inp):
    d = {}
    d["sg_blk"] = _col_blocks(inp["L0_ffn_shared_gate"])
    d["su_blk"] = _col_blocks(inp["L0_ffn_shared_up"])
    d["sd_blk"] = _col_blocks(inp["L0_ffn_shared_down"])
    d["wg_blk"] = np.stack([_col_blocks(inp["L0_ffn_w_gate"][e]) for e in range(E)])
    d["wu_blk"] = np.stack([_col_blocks(inp["L0_ffn_w_up"][e]) for e in range(E)])
    d["wd_blk"] = np.stack([_col_blocks(inp["L0_ffn_w_down"][e]) for e in range(E)])
    d["wk_slab"] = _slab_pieces(inp["L0_attn_Wk"])
    d["wv_slab"] = _slab_pieces(inp["L0_attn_Wv"])
    d["wq_blk"] = _col_blocks(inp["L0_attn_Wq"])
    d["wo_blk"] = _col_blocks(inp["L0_attn_Wo"])
    rwT = np.ascontiguousarray(np.asarray(inp["L0_ffn_router_w"]).T)
    d["rw_f32"] = np.ascontiguousarray(rwT.reshape(16, 128, 8).transpose(1, 0, 2).reshape(128, 128))
    d["bq_pp"] = np.ascontiguousarray(np.asarray(inp["L0_attn_bq"]).reshape(16, 128).T)
    d["bo_pp"] = np.ascontiguousarray(np.asarray(inp["L0_attn_bo"]).reshape(16, 128).T)
    d["bk_row"] = np.ascontiguousarray(np.asarray(inp["L0_attn_bk"]).reshape(1, D))
    d["bv_row"] = np.ascontiguousarray(np.asarray(inp["L0_attn_bv"]).reshape(1, D))
    return {k: np.ascontiguousarray(np.asarray(v, dtype=np.float32)) for k, v in d.items()}


def build_nc(ph1_tiles=NTILES, ph2_tiles=NTILES, debug_stop=None):
    nc = bass.Bass("TRN2", target_bir_lowering=False)
    P = Prog()

    x_fm = nc.dram_tensor("x_fm", [D, SEQ], F32, kind="ExternalInput").ap()
    out_fm = nc.dram_tensor("out_fm", [D, SEQ], F32, kind="ExternalOutput").ap()
    x1_fm = nc.dram_tensor("x1_fm", [D, SEQ], F32, kind="Internal").ap()
    dbg_b = nc.dram_tensor("dbg_b", [128, KC * NT], BF16, kind="ExternalOutput").ap()

    w_in, w_b16 = {}, {}
    def wdecl(name, shape):
        w_in[name] = nc.dram_tensor(name, list(shape), F32, kind="ExternalInput").ap()
        w_b16[name] = nc.dram_tensor(name + "_b", list(shape), BF16, kind="Internal").ap()
    wdecl("sg_blk", (IC, 128, KC * 128))
    wdecl("su_blk", (IC, 128, KC * 128))
    wdecl("sd_blk", (KC, 128, IC * 128))
    wdecl("wg_blk", (E, IC, 128, KC * 128))
    wdecl("wu_blk", (E, IC, 128, KC * 128))
    wdecl("wd_blk", (E, KC, 128, IC * 128))
    wdecl("wk_slab", (4, 4, 128, 2048))
    wdecl("wv_slab", (4, 4, 128, 2048))
    wdecl("wq_blk", (KC, 128, KC * 128))
    wdecl("wo_blk", (KC, 128, KC * 128))
    rw_hbm = nc.dram_tensor("rw_f32", [128, 128], F32, kind="ExternalInput").ap()
    bq_hbm = nc.dram_tensor("bq_pp", [128, KC], F32, kind="ExternalInput").ap()
    bo_hbm = nc.dram_tensor("bo_pp", [128, KC], F32, kind="ExternalInput").ap()
    bk_hbm = nc.dram_tensor("bk_row", [1, D], F32, kind="ExternalInput").ap()
    bv_hbm = nc.dram_tensor("bv_row", [1, D], F32, kind="ExternalInput").ap()

    ctx = ExitStack()
    SB = lambda name, shape, dt: ctx.enter_context(nc.sbuf_tensor(name, shape, dt))
    PSA = lambda name, shape: ctx.enter_context(nc.psum_tensor(name, shape, F32))

    stream = SB("stream", [128, KC * NT], F32)
    lnb = SB("lnb", [128, KC * NT], BF16)
    he = SB("he", [128, IC * NT], BF16)
    sig = SB("sig", [128, 2 * NT], BF16)
    uesc = SB("uesc", [128, 2 * NT], BF16)
    v_scr = SB("v_scr", [128, 2 * NT], F32)
    sq_scr = SB("sq_scr", [128, 2 * NT], BF16)
    qtmp = SB("qtmp", [128, 2 * NT], F32)
    z_scr = SB("z_scr", [128, 2 * NT], F32)
    ktm = SB("ktm", [128, TC * D], BF16)
    vtm = SB("vtm", [128, TC * NH * 129], BF16)
    kv_acc = SB("kv_acc", [128, NH * 129], F32)
    kv_b16 = SB("kv_b16", [128, NH * 129], BF16)
    ksrep = SB("ksrep", [128, NH * 128], BF16)
    comb_tm = SB("comb_tm", [128, TC * 8], BF16)
    combT = SB("combT", [8, NT], BF16)
    comb_bc = SB("comb_bc", [128, E * NT], BF16)
    ring = SB("ring", [128, RING_SLOTS * SLOT_COLS], BF16)
    rw_sb = SB("rw_sb", [128, 128], F32)
    wr_row = SB("wr_row", [1, 8], F32)
    wrsum = SB("wrsum", [128, 8], F32)
    bq_sb = SB("bq_sb", [128, KC], F32)
    bo_sb = SB("bo_sb", [128, KC], F32)
    bk_sb = SB("bk_sb", [1, D], F32)
    bv_sb = SB("bv_sb", [1, D], F32)
    mcols = SB("mcols", [128, TC], F32)
    nmcols = SB("nmcols", [128, TC], F32)
    rcols = SB("rcols", [128, TC], F32)
    m2save = SB("m2save", [128, TC * NTILES], F32)
    r2save = SB("r2save", [128, TC * NTILES], F32)
    rep_sb = SB("rep_sb", [128, 2 * 128], F32)
    lrow = SB("lrow", [128, 8], F32)
    lcol = SB("lcol", [128, 8], F32)
    mask8 = SB("mask8", [128, 8], F32)
    em8 = SB("em8", [128, 8], F32)
    decay_cols = SB("decay_cols", [128, SEQ // 128], F32)
    decay_bc = SB("decay_bc", [128, SEQ], BF16)
    rep_b = SB("rep_b", [128, 128], BF16)
    ones_col = SB("ones_col", [128, 1], F32)
    ones_r1f = SB("ones_r1f", [1, 128], F32)
    ones_r1b = SB("ones_r1b", [1, 128], BF16)
    ones128f = SB("ones128f", [128, 128], F32)
    ones128b = SB("ones128b", [128, 128], BF16)
    ones_r512b = SB("ones_r512b", [1, NT], BF16)
    eps_r1b = SB("eps_r1b", [1, 128], BF16)
    idf32 = SB("idf32", [128, 128], F32)
    idb16 = SB("idb16", [128, 128], BF16)
    iota_t = SB("iota_t", [128, 128], F32)
    pcol = SB("pcol", [128, 1], F32)
    epsln_col = SB("epsln_col", [128, 1], F32)
    one1 = SB("one1", [1, 1], F32)
    ones_col_b = SB("ones_col_b", [128, 1], BF16)
    mrow = SB("mrow", [1, NT], F32)
    mean_sb = SB("mean_sb", [128, NT], F32)
    rstd_sb = SB("rstd_sb", [128, NT], F32)
    wrow = SB("wrow", [1, NT], F32)
    bias_bck = SB("bias_bck", [128, D], BF16)
    bias_bcv = SB("bias_bcv", [128, D], BF16)
    iota8 = SB("iota8", [8, E * 128], BF16)
    sel_all = SB("sel_all", [8, E * 128], BF16)

    ps = [PSA(f"ps{i}", [128, NT]) for i in range(4)]
    pkv = [PSA(f"pkv{i}", [128, NT]) for i in range(2)]
    pbc = [PSA(f"pbc{i}", [128, NT]) for i in range(2)]
    ps_r = [Res() for _ in range(4)]
    pkv_r = [Res() for _ in range(2)]
    pbc_r = [Res() for _ in range(2)]
    sig_r = [Res(), Res()]
    sq_r = [Res(), Res()]
    qtmp_r = [Res(), Res()]
    v_r = [Res(), Res()]
    rep_r = [Res(), Res()]
    ktm_r = Res()
    vtm_r = Res()
    lnb_r = Res()
    mean_sb_r = Res()
    rows_r = Res()

    s_pe = P.sem("pe"); s_dve = P.sem("dve"); s_act = P.sem("act"); s_pool = P.sem("pool")
    s_ring = [P.sem(f"ring{i}") for i in range(RING_SLOTS)]
    s_dx = P.sem("dx")
    s_dx1st = P.sem("dx1st")
    s_dout = P.sem("dout")
    s_dcast = P.sem("dcast")
    s_dxg = [P.sem(f"dxg{g}") for g in range(4)]
    P.auto = {"vector": s_dve, "scalar": s_act, "gpsimd": s_pool}
    P.dma_sems = set([s_dx, s_dx1st, s_dout, s_dcast] + s_ring + s_dxg)

    def acq(eng, res):
        if res.free is not None:
            P.wait(eng, res.free[0], res.free[1])

    ring_ns = {"next": 0, "free": [0] * RING_SLOTS}

    def ring_load(hbm_ap, cast_key):
        s = ring_ns["next"]
        ring_ns["next"] = (s + 1) % RING_SLOTS
        P.wait("sync", s_pe, ring_ns["free"][s])
        P.wait_ms("sync", cast_ms[cast_key])
        ncols = hbm_ap.shape[-1]
        slot_ap = ring[:, s * SLOT_COLS: s * SLOT_COLS + ncols]
        lv = P.emit("sync", lambda e, o=slot_ap, i=hbm_ap: e.dma_start(out=o, in_=i),
                    inc=s_ring[s], amt=16)
        return s, lv

    def ring_free(slot, pe_ms):
        ring_ns["free"][slot] = pe_ms

    for dst, src in ((rw_sb, rw_hbm), (bq_sb, bq_hbm), (bo_sb, bo_hbm),
                     (bk_sb, bk_hbm), (bv_sb, bv_hbm)):
        P.emit("scalar", lambda e, d=dst, s=src: e.dma_start(out=d[:], in_=s),
               inc=s_dx, amt=16)
    consts_ld = P.total(s_dx)
    for eng in ("tensor", "vector", "scalar"):
        P.wait(eng, s_dx, consts_ld)

    P.emit("gpsimd", lambda e: e.iota(iota_t[:], [[1, 128]], channel_multiplier=0,
                                      allow_small_or_imprecise_dtypes=True))
    P.emit("gpsimd", lambda e: e.iota(pcol[:], [[0, 1]], channel_multiplier=1,
                                      allow_small_or_imprecise_dtypes=True))
    P.emit("gpsimd", lambda e: e.iota(decay_cols[:], [[128, SEQ // 128]], channel_multiplier=1,
                                      allow_small_or_imprecise_dtypes=True))
    P.emit("gpsimd", lambda e: e.memset(ones_col[:], 1.0))
    P.emit("gpsimd", lambda e: e.memset(ones_r1f[:], 1.0))
    P.emit("gpsimd", lambda e: e.memset(ones_r1b[:], 1.0))
    P.emit("gpsimd", lambda e: e.memset(ones128f[:], 1.0))
    P.emit("gpsimd", lambda e: e.memset(ones128b[:], 1.0))
    P.emit("gpsimd", lambda e: e.memset(ones_r512b[:], 1.0))
    P.emit("gpsimd", lambda e: e.memset(eps_r1b[:], ATT_EPS))
    P.emit("gpsimd", lambda e: e.memset(kv_acc[:], 0.0))
    P.emit("gpsimd", lambda e: e.memset(epsln_col[:], LN_EPS))
    P.emit("gpsimd", lambda e: e.memset(one1[:], 1.0))
    P.emit("gpsimd", lambda e: e.memset(ones_col_b[:], 1.0))
    P.emit("gpsimd", lambda e: e.iota(iota8[:], [[1, E], [0, 128]], channel_multiplier=0,
                                      allow_small_or_imprecise_dtypes=True))
    for c in range(TC):
        for h in range(NH):
            off = c * NH * 129 + h * 129 + 128
            P.emit("gpsimd", lambda e, off=off: e.memset(vtm[:, off:off + 1], 1.0))
    pool_init = P.emit("gpsimd", lambda e: e.memset(em8[:], 0.0), inc=s_pool)

    P.wait("vector", s_pool, pool_init)
    P.emit("vector", lambda e: e.tensor_scalar(idf32[:], iota_t[:], pcol[:], None, ALU.is_equal))
    P.emit("vector", lambda e: e.tensor_scalar(idb16[:], iota_t[:], pcol[:], None, ALU.is_equal))
    id_ms = P.emit("vector", lambda e: e.tensor_scalar(sel_all[:], iota8[:], pcol[0:8, :], None, ALU.is_equal),
                   inc=s_dve)
    P.wait("scalar", s_pool, pool_init)
    dec_ms = P.emit("scalar", lambda e: e.activation(decay_cols[:], decay_cols[:], AF.Exp,
                                                     scale=-1.0 / SEQ), inc=s_act)
    P.wait("vector", s_act, dec_ms)
    P.wait("tensor", s_dve, id_ms)
    rep_b_r = Res()
    for gc in range(SEQ // 128):
        pb = pbc[gc % 2]
        acq("vector", rep_b_r)
        dv = P.emit("vector", lambda e, gc=gc: e.tensor_scalar(
            rep_b[:], ones128b[:], decay_cols[:, gc:gc + 1], None, ALU.mult), inc=s_dve)
        P.wait("tensor", s_dve, dv)
        acq("tensor", pbc_r[gc % 2])
        pm = P.emit("tensor", lambda e, pb=pb: e.matmul(
            pb[:, 0:128], rep_b[:], idb16[:], start=True, stop=True), inc=s_pe)
        rep_b_r.free = (s_pe, pm)
        P.wait("vector", s_pe, pm)
        dvc = P.emit("vector", lambda e, gc=gc, pb=pb: e.tensor_copy(
            out=decay_bc[:, gc * 128:(gc + 1) * 128], in_=pb[:, 0:128]), inc=s_dve)
        (pbc_r[gc % 2]).free = (s_dve, dvc)

    P.wait("tensor", s_dx, consts_ld)
    acq("tensor", pkv_r[0])
    for k in range(KC):
        mmi = P.emit("tensor", lambda e, k=k: e.matmul(
            pkv[0][0:1, 0:8], ones_col[:], rw_sb[:, k * 8:(k + 1) * 8],
            start=(k == 0), stop=(k == KC - 1)), inc=(s_pe if k == KC - 1 else None))
    P.wait("vector", s_pe, mmi)
    wrr_ms = P.emit("vector", lambda e: e.tensor_copy(out=wr_row[:], in_=pkv[0][0:1, 0:8]),
                    inc=s_dve)
    pkv_r[0].free = (s_dve, wrr_ms)
    P.wait("tensor", s_dve, wrr_ms)
    acq("tensor", pkv_r[1])
    pm = P.emit("tensor", lambda e: e.matmul(pkv[1][:, 0:8], ones_r1f[:], wr_row[:],
                                             start=True, stop=True), inc=s_pe)
    P.wait("vector", s_pe, pm)
    wrs_ms = P.emit("vector", lambda e: e.tensor_copy(out=wrsum[:], in_=pkv[1][:, 0:8]),
                    inc=s_dve)
    pkv_r[1].free = (s_dve, wrs_ms)

    for brow_sb, bdst in ((bk_sb, bias_bck), (bv_sb, bias_bcv)):
        for seg in range(4):
            pb = pbc[seg % 2]
            acq("tensor", pbc_r[seg % 2])
            pm = P.emit("tensor", lambda e, pb=pb, brow_sb=brow_sb, seg=seg: e.matmul(
                pb[:], ones_r1f[:], brow_sb[:, seg * NT:(seg + 1) * NT],
                start=True, stop=True), inc=s_pe)
            P.wait("vector", s_pe, pm)
            dv = P.emit("vector", lambda e, pb=pb, bdst=bdst, seg=seg: e.tensor_copy(
                out=bdst[:, seg * NT:(seg + 1) * NT], in_=pb[:]), inc=s_dve)
            pbc_r[seg % 2].free = (s_dve, dv)

    cast_ms = {}
    cast_order = [("sg_blk",), ("su_blk",), ("sd_blk",)]
    for e_ in range(E):
        cast_order += [("wg_blk", e_), ("wu_blk", e_), ("wd_blk", e_)]
    cast_order += [("wk_slab",), ("wv_slab",), ("wq_blk",), ("wo_blk",)]
    for ci, key in enumerate(cast_order):
        name = key[0]
        src = w_in[name] if len(key) == 1 else w_in[name][key[1]]
        dst = w_b16[name] if len(key) == 1 else w_b16[name][key[1]]
        csem = P.sem(f"cast{ci}")
        P.dma_sems.add(csem)
        P.emit("gpsimd", lambda e, s=src, d=dst: e.dma_start(out=d, in_=s),
               inc=csem, amt=16)
        cast_ms[key] = (csem, 16)

    rot = {"i": 0}
    lnb_ms = {}
    he_ms = {}
    stream_ms = {}
    kv_ev = {}
    q_ms = {}
    att_ms = {}

    def next_ps(eng="tensor"):
        i = rot["i"] % 4
        rot["i"] += 1
        acq(eng, ps_r[i])
        return ps[i], ps_r[i]

    def wait_lnb(eng):
        for k in range(KC):
            P.wait_ms(eng, lnb_ms[k])

    def ln_stats(t, save_stats, need_cols=True, chunk_ready=None):
        pstat, pstat_r = ps[0], ps_r[0]
        pstatq, pstatq_r = ps[1], ps_r[1]
        acq("tensor", pstat_r)
        acq("tensor", pstatq_r)
        acq("gpsimd", lnb_r)
        for k in range(KC):
            sl = k % 2
            if chunk_ready is not None:
                P.wait_ms("gpsimd", chunk_ready[k])
                P.wait_ms("scalar", chunk_ready[k])
            if k in lnb_ms:
                P.wait_ms("gpsimd", lnb_ms[k])
            xb_ms = P.emit("gpsimd", lambda e, k=k: e.tensor_copy(
                out=lnb[:, k * NT:(k + 1) * NT], in_=stream[:, k * NT:(k + 1) * NT]))
            acq("scalar", sq_r[sl])
            sq_ms = P.emit("scalar", lambda e, k=k, sl=sl: e.activation(
                sq_scr[:, sl * NT:(sl + 1) * NT], stream[:, k * NT:(k + 1) * NT], AF.Square),
                inc=s_act)
            P.wait("tensor", s_pool, xb_ms)
            P.wait("tensor", s_act, sq_ms)
            P.emit("tensor", lambda e, k=k: e.matmul(
                pstat[0:1, 0:NT], ones_col_b[:], lnb[:, k * NT:(k + 1) * NT],
                start=(k == 0), stop=(k == KC - 1)))
            pe_ms = P.emit("tensor", lambda e, k=k, sl=sl: e.matmul(
                pstatq[0:1, 0:NT], ones_col_b[:], sq_scr[:, sl * NT:(sl + 1) * NT],
                start=(k == 0), stop=(k == KC - 1)), inc=s_pe)
            sq_r[sl].free = (s_pe, pe_ms)
        stats_pe = pe_ms
        P.wait("vector", s_pe, stats_pe)
        acq("vector", rows_r)
        P.emit("vector", lambda e: e.tensor_scalar(mrow[:], pstat[0:1, 0:NT], 1.0 / D, None, ALU.mult))
        wr_ms0 = P.emit("vector", lambda e: e.tensor_scalar(wrow[:], pstatq[0:1, 0:NT], 1.0 / D, None, ALU.mult),
                        inc=s_dve)
        P.wait("scalar", s_dve, wr_ms0)
        m2_ms = P.emit("scalar", lambda e: e.activation(z_scr[0:1, 0:NT], mrow[:], AF.Square),
                       inc=s_act)
        P.wait("vector", s_act, m2_ms)
        vr_ms = P.emit("vector", lambda e: e.tensor_tensor(wrow[:], wrow[:], z_scr[0:1, 0:NT], ALU.subtract),
                       inc=s_dve)
        P.wait("scalar", s_dve, vr_ms)
        P.emit("scalar", lambda e: e.activation(wrow[:], wrow[:], AF.Ln, bias=epsln_col[0:1, :]))
        rstd_ms = P.emit("scalar", lambda e: e.activation(wrow[:], wrow[:], AF.Exp, scale=-0.5),
                         inc=s_act)
        pstatq_r.free = (s_dve, wr_ms0)
        _ln_state["vr_ms"] = vr_ms
        _ln_state["rstd_ms"] = rstd_ms
        _ln_state["pstat"] = pstat
        _ln_state["pstat_r"] = pstat_r
        return rstd_ms

    _ln_state = {}

    def ln_cols(t, save_stats=False):
        pstat, pstat_r = _ln_state["pstat"], _ln_state["pstat_r"]
        P.wait("tensor", s_dve, _ln_state["vr_ms"])
        P.wait("tensor", s_act, _ln_state["rstd_ms"])
        for c in range(TC):
            P.emit("tensor", lambda e, c=c, pstat=pstat: e.transpose(
                pstat[:, 64 + 4 * c: 65 + 4 * c], mrow[0:1, c * 128:(c + 1) * 128], one1[:]))
            pe_ms = P.emit("tensor", lambda e, c=c, pstat=pstat: e.transpose(
                pstat[:, 80 + 4 * c: 81 + 4 * c], wrow[0:1, c * 128:(c + 1) * 128], one1[:]),
                inc=(s_pe if c == TC - 1 else None))
        P.wait("vector", s_pe, pe_ms)
        for c in range(TC):
            P.emit("vector", lambda e, c=c, pstat=pstat: e.tensor_copy(
                out=mcols[:, c:c + 1], in_=pstat[:, 64 + 4 * c: 65 + 4 * c]))
            P.emit("vector", lambda e, c=c, pstat=pstat: e.tensor_copy(
                out=rcols[:, c:c + 1], in_=pstat[:, 80 + 4 * c: 81 + 4 * c]))
        nm_ms = P.emit("vector", lambda e: e.tensor_scalar(nmcols[:], mcols[:], -1.0, None, ALU.mult),
                       inc=s_dve)
        pstat_r.free = (s_dve, nm_ms)
        rows_r.free = (s_pe, pe_ms)
        if save_stats:
            P.emit("vector", lambda e, t=t: e.tensor_copy(out=m2save[:, t * TC:(t + 1) * TC], in_=mcols[:]))
            P.emit("vector", lambda e, t=t: e.tensor_copy(out=r2save[:, t * TC:(t + 1) * TC], in_=rcols[:]))

    def ln_finish(with_residual, ready_ms, chunk_ready=None):
        P.wait_ms("tensor", ready_ms)
        acq("tensor", pbc_r[0])
        P.emit("tensor", lambda e: e.matmul(pbc[0][:], ones_r1f[:], mrow[:],
                                            start=True, stop=True))
        acq("tensor", pbc_r[1])
        bc_ms = P.emit("tensor", lambda e: e.matmul(pbc[1][:], ones_r1f[:], wrow[:],
                                                    start=True, stop=True), inc=s_pe)
        P.wait("vector", s_pe, bc_ms)
        acq("vector", mean_sb_r)
        P.emit("vector", lambda e: e.tensor_copy(out=mean_sb[:], in_=pbc[0][:]))
        sbc_ms = P.emit("vector", lambda e: e.tensor_copy(out=rstd_sb[:], in_=pbc[1][:]),
                        inc=s_dve)
        return normalize(with_residual, bc_ms, sbc_ms, chunk_ready=chunk_ready)

    def broadcast_cols(ready_ms):
        P.wait_ms("vector", ready_ms)
        for c in range(TC):
            sl = c % 2
            acq("vector", rep_r[sl])
            dv = P.emit("vector", lambda e, c=c, sl=sl: e.tensor_scalar(
                rep_sb[:, sl * 128:(sl + 1) * 128], ones128f[:], mcols[:, c:c + 1], None, ALU.mult),
                inc=s_dve)
            P.wait("tensor", s_dve, dv)
            if c == 0:
                acq("tensor", pbc_r[0])
            pm = P.emit("tensor", lambda e, c=c, sl=sl: e.matmul(
                pbc[0][:, c * 128:(c + 1) * 128], rep_sb[:, sl * 128:(sl + 1) * 128], idf32[:],
                start=True, stop=True), inc=s_pe)
            rep_r[sl].free = (s_pe, pm)
            acq("vector", rep_r[sl])
            dv = P.emit("vector", lambda e, c=c, sl=sl: e.tensor_scalar(
                rep_sb[:, sl * 128:(sl + 1) * 128], ones128f[:], rcols[:, c:c + 1], None, ALU.mult),
                inc=s_dve)
            P.wait("tensor", s_dve, dv)
            if c == 0:
                acq("tensor", pbc_r[1])
            pm = P.emit("tensor", lambda e, c=c, sl=sl: e.matmul(
                pbc[1][:, c * 128:(c + 1) * 128], rep_sb[:, sl * 128:(sl + 1) * 128], idf32[:],
                start=True, stop=True), inc=s_pe)
            rep_r[sl].free = (s_pe, pm)
        return pm

    def normalize(with_residual, bc_ms, sbc_ms, chunk_ready=None):
        P.wait("vector", s_pe, bc_ms)
        P.wait("gpsimd", s_pe, bc_ms)
        last_dve = None
        last_pool = None
        for k in range(KC):
            on_dve = (k % 2 == 0)
            sl = 0 if on_dve else 1
            eng = "vector" if on_dve else "gpsimd"
            if chunk_ready is not None:
                P.wait_ms(eng, chunk_ready[k])
            if on_dve:
                P.emit("vector", lambda e, k=k, sl=sl: e.tensor_tensor(
                    v_scr[:, sl * NT:(sl + 1) * NT], stream[:, k * NT:(k + 1) * NT],
                    pbc[0][:], ALU.subtract))
                lm = P.emit("vector", lambda e, k=k, sl=sl: e.tensor_tensor(
                    lnb[:, k * NT:(k + 1) * NT], v_scr[:, sl * NT:(sl + 1) * NT],
                    pbc[1][:], ALU.mult), inc=s_dve)
                last_dve = lm
                lnb_ms[k] = (s_dve, lm)
            else:
                P.wait_ms("gpsimd", (s_dve, sbc_ms))
                P.emit("gpsimd", lambda e, k=k, sl=sl: e.tensor_tensor(
                    v_scr[:, sl * NT:(sl + 1) * NT], stream[:, k * NT:(k + 1) * NT],
                    mean_sb[:], ALU.subtract))
                lm = P.emit("gpsimd", lambda e, k=k, sl=sl: e.tensor_tensor(
                    lnb[:, k * NT:(k + 1) * NT], v_scr[:, sl * NT:(sl + 1) * NT],
                    rstd_sb[:], ALU.mult), inc=s_pool)
                last_pool = lm
                lnb_ms[k] = (s_pool, lm)
        pbc_r[0].free = (s_dve, last_dve)
        pbc_r[1].free = (s_dve, last_dve)
        if last_pool is not None:
            mean_sb_r.free = (s_pool, last_pool)
        return last_dve

    def residual_update(gate_ms=None):
        last = None
        for k in range(KC):
            P.wait_ms("vector", lnb_ms[k])
            if gate_ms is not None:
                P.wait_ms("vector", gate_ms)
            last = P.emit("vector", lambda e, k=k: e.scalar_tensor_tensor(
                stream[:, k * NT:(k + 1) * NT], lnb[:, k * NT:(k + 1) * NT], 0.5,
                stream[:, k * NT:(k + 1) * NT], ALU.mult, ALU.add), inc=s_dve)
        return last

    def router_logits(t):
        acq("tensor", pkv_r[1])
        gm = None
        for c in range(TC):
            for k in range(KC):
                gm = P.emit("tensor", lambda e, c=c, k=k: e.matmul(
                    pkv[1][:, c * 8:(c + 1) * 8],
                    stream[:, k * NT + c * 128: k * NT + (c + 1) * 128],
                    rw_sb[:, k * 8:(k + 1) * 8],
                    start=(k == 0), stop=(k == KC - 1)),
                    inc=(s_pe if (k == KC - 1 and c == TC - 1) else None))
        return gm

    def router_comb(t, gm, nmc, rc):
        ctv = None
        for c in range(TC):
            P.wait("vector", s_pe, gm)
            P.wait("vector", s_dve, wrs_ms)
            P.emit("vector", lambda e, c=c, nmc=nmc: e.scalar_tensor_tensor(
                lrow[:], wrsum[:], nmc[:, c:c + 1], pkv[1][:, c * 8:(c + 1) * 8],
                ALU.mult, ALU.add))
            P.emit("vector", lambda e: e.tensor_reduce(
                lcol[:, 0:1], lrow[:], axis=AX, op=ALU.max, negate=True))
            hm = P.emit("vector", lambda e, c=c, rc=rc: e.tensor_scalar(
                lrow[:], lrow[:], lcol[:, 0:1], rc[:, c:c + 1], ALU.add, ALU.mult),
                inc=s_dve)
            P.wait("scalar", s_dve, hm)
            em = P.emit("scalar", lambda e: e.activation(lrow[:], lrow[:], AF.Exp), inc=s_act)
            P.wait("vector", s_act, em)
            P.emit("vector", lambda e: e.tensor_reduce(lcol[:, 1:2], lrow[:], axis=AX, op=ALU.add))
            P.emit("vector", lambda e: e.reciprocal(lcol[:, 2:3], lcol[:, 1:2]))
            P.emit("vector", lambda e: e.tensor_reduce(lcol[:, 3:4], lrow[:], axis=AX, op=ALU.max))
            P.emit("vector", lambda e: e.tensor_scalar(mask8[:], lrow[:], lcol[:, 3:4], None, ALU.is_ge))
            P.emit("vector", lambda e: e.scalar_tensor_tensor(em8[:], mask8[:], -1e30, lrow[:],
                                                              ALU.mult, ALU.add))
            P.emit("vector", lambda e: e.tensor_reduce(lcol[:, 5:6], em8[:], axis=AX, op=ALU.max))
            P.emit("vector", lambda e: e.tensor_scalar(mask8[:], lrow[:], lcol[:, 5:6], None, ALU.is_ge))
            cm = P.emit("vector", lambda e, c=c: e.scalar_tensor_tensor(
                comb_tm[:, c * 8:(c + 1) * 8], mask8[:], lcol[:, 2:3], lrow[:],
                ALU.mult, ALU.mult), inc=s_dve)
            if c == TC - 1:
                pkv_r[1].free = (s_dve, cm)
            P.wait("tensor", s_dve, cm)
            if c == 0:
                acq("tensor", pkv_r[0])
            pm = P.emit("tensor", lambda e, c=c: e.matmul(
                pkv[0][0:8, c * 128:(c + 1) * 128], comb_tm[:, c * 8:(c + 1) * 8], idb16[:],
                start=True, stop=True), inc=s_pe)
            P.wait("vector", s_pe, pm)
            ctv = P.emit("vector", lambda e, c=c: e.tensor_copy(
                out=combT[:, c * 128:(c + 1) * 128], in_=pkv[0][0:8, c * 128:(c + 1) * 128]),
                inc=s_dve)
        pkv_r[0].free = (s_dve, ctv)
        P.wait("tensor", s_dve, ctv)
        last = None
        for ee in range(E):
            pb, pr = pbc[ee % 2], pbc_r[ee % 2]
            acq("tensor", pr)
            pm = P.emit("tensor", lambda e, ee=ee, pb=pb: e.matmul(
                pb[:], sel_all[:, ee * 128:(ee + 1) * 128], combT[:], start=True, stop=True), inc=s_pe)
            P.wait("vector", s_pe, pm)
            last = P.emit("vector", lambda e, ee=ee, pb=pb: e.tensor_copy(
                out=comb_bc[:, ee * NT:(ee + 1) * NT], in_=pb[:]), inc=s_dve)
            pr.free = (s_dve, last)
        return last

    def ffn_stage(t, comb_ready, experts=None):
        last_sm = None
        if experts is None:
            experts = [None] + list(range(E))
        for ex in experts:
            is_shared = ex is None
            for c in range(IC):
                gsrc = w_b16["sg_blk"][c] if is_shared else w_b16["wg_blk"][ex][c]
                usrc = w_b16["su_blk"][c] if is_shared else w_b16["wu_blk"][ex][c]
                gkey = ("sg_blk",) if is_shared else ("wg_blk", ex)
                ukey = ("su_blk",) if is_shared else ("wu_blk", ex)
                pg, pgr = next_ps()
                pu, pur = next_ps()
                for src, key, pp in ((gsrc, gkey, pg), (usrc, ukey, pu)):
                    slot, lv = ring_load(src, key)
                    P.wait("tensor", s_ring[slot], lv)
                    for k in range(KC):
                        P.wait_ms("tensor", lnb_ms[k])
                        mm = P.emit("tensor", lambda e, slot=slot, k=k, pp=pp: e.matmul(
                            pp[:], ring[:, slot * SLOT_COLS + k * 128: slot * SLOT_COLS + (k + 1) * 128],
                            lnb[:, k * NT:(k + 1) * NT], start=(k == 0), stop=(k == KC - 1)),
                            inc=(s_pe if k == KC - 1 else None))
                    ring_free(slot, mm)
                g_pe, u_pe = None, mm
                P.wait("scalar", s_pe, u_pe)
                sl = c % 2
                acq("scalar", sig_r[sl])
                am = P.emit("scalar", lambda e, pg=pg, sl=sl: e.activation(
                    sig[:, sl * NT:(sl + 1) * NT], pg[:], AF.Sigmoid), inc=s_act)
                P.wait("vector", s_act, am)
                if is_shared:
                    hm = P.emit("vector", lambda e, c=c, pu=pu, sl=sl: e.tensor_tensor(
                        he[:, c * NT:(c + 1) * NT], sig[:, sl * NT:(sl + 1) * NT], pu[:],
                        ALU.mult), inc=s_dve)
                else:
                    P.wait_ms("vector", comb_ready)
                    P.emit("vector", lambda e, pu=pu, sl=sl, ex=ex: e.tensor_tensor(
                        uesc[:, sl * NT:(sl + 1) * NT], pu[:],
                        comb_bc[:, ex * NT:(ex + 1) * NT], ALU.mult))
                    hm = P.emit("vector", lambda e, c=c, sl=sl: e.tensor_tensor(
                        he[:, c * NT:(c + 1) * NT], sig[:, sl * NT:(sl + 1) * NT],
                        uesc[:, sl * NT:(sl + 1) * NT], ALU.mult), inc=s_dve)
                sig_r[sl].free = (s_dve, hm)
                pgr.free = (s_dve, hm)
                pur.free = (s_dve, hm)
                he_ms[c] = hm
            for oc in range(KC):
                dsrc = w_b16["sd_blk"][oc] if is_shared else w_b16["wd_blk"][ex][oc]
                dkey = ("sd_blk",) if is_shared else ("wd_blk", ex)
                pd, pdr = next_ps()
                slot, lv = ring_load(dsrc, dkey)
                P.wait("tensor", s_ring[slot], lv)
                for ic in range(IC):
                    P.wait("tensor", s_dve, he_ms[ic])
                    mm = P.emit("tensor", lambda e, slot=slot, ic=ic, pd=pd: e.matmul(
                        pd[:], ring[:, slot * SLOT_COLS + ic * 128: slot * SLOT_COLS + (ic + 1) * 128],
                        he[:, ic * NT:(ic + 1) * NT], start=(ic == 0), stop=(ic == IC - 1)),
                        inc=(s_pe if ic == IC - 1 else None))
                ring_free(slot, mm)
                P.wait("vector", s_pe, mm)
                sm = P.emit("vector", lambda e, oc=oc, pd=pd: e.scalar_tensor_tensor(
                    stream[:, oc * NT:(oc + 1) * NT], pd[:], 0.5,
                    stream[:, oc * NT:(oc + 1) * NT], ALU.mult, ALU.add), inc=s_dve)
                pdr.free = (s_dve, sm)
                stream_ms[oc] = sm
                last_sm = sm
        return last_sm

    def kv_stage(t):
        for mat, bias_row in (("wk_slab", bk_sb), ("wv_slab", bv_sb)):
            for dseg in range(4):
                for c in range(TC):
                    acq("tensor", ps_r[c])
                for kg in range(4):
                    src = w_b16[mat][dseg][kg]
                    slot, lv = ring_load(src, (mat,))
                    P.wait("tensor", s_ring[slot], lv)
                    for c in range(TC):
                        for kk in range(4):
                            k = kg * 4 + kk
                            P.wait_ms("tensor", lnb_ms[k])
                            mm = P.emit("tensor", lambda e, slot=slot, kk=kk, k=k, c=c, kg=kg: e.matmul(
                                ps[c][:], lnb[:, k * NT + c * 128: k * NT + (c + 1) * 128],
                                ring[:, slot * SLOT_COLS + kk * 512: slot * SLOT_COLS + (kk + 1) * 512],
                                start=(kg == 0 and kk == 0), stop=(kg == 3 and kk == 3)),
                                inc=(s_pe if (c == TC - 1 and kk == 3) else None))
                    ring_free(slot, mm)
                for c in range(TC):
                    pe_ms = mm if c == TC - 1 else None
                    P.wait("vector" if mat == "wv_slab" else "vector", s_pe, mm)
                    if mat == "wk_slab":
                        bb = bias_bck
                        sl = c % 2
                        acq("vector", qtmp_r[sl])
                        dv = P.emit("vector", lambda e, c=c, dseg=dseg, sl=sl, bb=bb: e.tensor_tensor(
                            qtmp[:, sl * NT:(sl + 1) * NT], ps[c][:],
                            bb[:, dseg * 512:(dseg + 1) * 512], ALU.add), inc=s_dve)
                        ps_r[c].free = (s_dve, dv)
                        P.wait("scalar", s_dve, dv)
                        if t == 0 and dseg == 0 and c == 0:
                            acq("scalar", ktm_r)
                        ev = P.emit("scalar", lambda e, t=t, c=c, dseg=dseg, sl=sl: e.activation(
                            ktm[:, c * D + dseg * 512: c * D + (dseg + 1) * 512],
                            qtmp[:, sl * NT:(sl + 1) * NT],
                            AF.Relu, scale=decay_cols[:, t * TC + c: t * TC + c + 1]), inc=s_act)
                        qtmp_r[sl].free = (s_act, ev)
                        kv_ev[(mat, dseg, c)] = (s_act, ev)
                    else:
                        if t == 0 and dseg == 0 and c == 0:
                            acq("vector", vtm_r)
                        base = c * NH * 129 + dseg * 4 * 129
                        dst_ap = vtm[:, base: base + 4 * 129].rearrange(
                            "p (h d) -> p h d", h=4)[:, :, 0:128]
                        ev = P.emit("vector", lambda e, c=c, dseg=dseg, dst_ap=dst_ap: e.tensor_tensor(
                            out=dst_ap, in0=ps[c][:].rearrange("p (h d) -> p h d", h=4),
                            in1=bias_bcv[:, dseg * 512:(dseg + 1) * 512].rearrange("p (h d) -> p h d", h=4),
                            op=ALU.add),
                            inc=s_dve)
                        kv_ev[(mat, dseg, c)] = (s_dve, ev)
                        ps_r[c].free = (s_dve, ev)
        lnb_r.free = (s_pe, P.total(s_pe))
        return None

    def kv_accum(t, heads=range(NH)):
        last_dv = None
        for h in heads:
            dseg = h // 4
            for c in range(TC):
                P.wait_ms("tensor", kv_ev[("wk_slab", dseg, c)])
                P.wait_ms("tensor", kv_ev[("wv_slab", dseg, c)])
            pp, pr = pkv[h % 2], pkv_r[h % 2]
            acq("tensor", pr)
            for c in range(TC):
                mm = P.emit("tensor", lambda e, h=h, c=c, pp=pp: e.matmul(
                    pp[:, 0:129], ktm[:, c * D + h * 128: c * D + (h + 1) * 128],
                    vtm[:, c * NH * 129 + h * 129: c * NH * 129 + (h + 1) * 129],
                    start=(c == 0), stop=(c == TC - 1)),
                    inc=(s_pe if c == TC - 1 else None))
            P.wait("vector", s_pe, mm)
            last_dv = P.emit("vector", lambda e, h=h, pp=pp: e.tensor_tensor(
                kv_acc[:, h * 129:(h + 1) * 129], pp[:, 0:129],
                kv_acc[:, h * 129:(h + 1) * 129], ALU.add), inc=s_dve)
            pr.free = (s_dve, last_dv)
        if list(heads)[-1] == NH - 1:
            ktm_r.free = (s_pe, P.total(s_pe))
            vtm_r.free = (s_pe, P.total(s_pe))
            lnb_r.free = (s_pe, P.total(s_pe))
        return last_dv

    x1_store_ms = [None] * NTILES
    for t in range(ph1_tiles):
        if t > 0:
            P.wait("scalar", s_dx1st, x1_store_ms[t - 1])
        xl_gms = {}
        for g in range(4):
            xl = P.emit("scalar", lambda e, t=t, g=g: e.dma_start(
                out=stream[:].rearrange("p (k n) -> p k n", k=KC)[:, 4 * g:4 * g + 4, :],
                in_=x_fm[:, t * NT:(t + 1) * NT].rearrange(
                    "(k p) n -> p k n", p=128)[:, 4 * g:4 * g + 4, :]),
                inc=s_dxg[g], amt=16)
            for k in range(4 * g, 4 * g + 4):
                xl_gms[k] = (s_dxg[g], xl)
        rms = ln_stats(t, save_stats=False, chunk_ready=xl_gms)
        if debug_stop == "stats":
            P.wait("gpsimd", s_act, rms)
            P.emit("gpsimd", lambda e: e.dma_start(out=dbg_b[:, 0:TC], in_=mcols[:]), inc=s_dout, amt=16)
            P.emit("gpsimd", lambda e: e.dma_start(out=dbg_b[:, TC:2 * TC], in_=rcols[:]), inc=s_dout, amt=16)
            P.emit("gpsimd", lambda e: e.dma_start(out=dbg_b[:, 2 * TC:5 * TC], in_=vcols[:]), inc=s_dout, amt=16)
            P.emit("gpsimd", lambda e: e.dma_start(out=dbg_b[:, 5 * TC:6 * TC], in_=nmcols[:]), inc=s_dout, amt=16)
            P.wait("sync", s_dout, P.total(s_dout))
            _finish(nc, ctx, P)
            return nc
        ln_finish(True, (s_act, rms))
        if t > 0:
            kv_accum(t - 1, heads=range(0, 8))
        gm = router_logits(t)
        ln_cols(t)
        residual_update((s_pe, P.total(s_pe)))
        if debug_stop in ("ln1", "router"):
            if debug_stop == "router":
                router_comb(t, gm, nmcols, rcols)
            P.wait("scalar", s_dve, P.total(s_dve))
            P.wait("scalar", s_pool, P.total(s_pool))
            src = lnb if debug_stop == "ln1" else comb_bc
            P.emit("scalar", lambda e, src=src: e.dma_start(
                out=dbg_b[:, 0:src.shape[-1]], in_=src[:]), inc=s_dout, amt=16)
            P.emit("scalar", lambda e, t=t: e.dma_start(
                out=out_fm[:, t * NT:(t + 1) * NT].rearrange("(k p) n -> p k n", p=128),
                in_=stream[:].rearrange("p (k n) -> p k n", k=KC)), inc=s_dout, amt=16)
            P.wait("sync", s_dout, P.total(s_dout))
            _finish(nc, ctx, P)
            return nc
        ffn_stage(t, None, experts=[None])
        if debug_stop != "ffnsh":
            comb_last = router_comb(t, gm, nmcols, rcols)
            ffn_stage(t, (s_dve, comb_last), experts=list(range(E)))
        lnb_r.free = (s_pe, P.total(s_pe))
        if debug_stop in ("ffnsh", "ffn1"):
            for oc in range(KC):
                P.wait("scalar", s_dve, stream_ms[oc])
            P.wait("scalar", s_dve, P.total(s_dve))
            P.emit("scalar", lambda e, t=t: e.dma_start(
                out=out_fm[:, t * NT:(t + 1) * NT].rearrange("(k p) n -> p k n", p=128),
                in_=stream[:].rearrange("p (k n) -> p k n", k=KC)), inc=s_dout, amt=16)
            P.wait("sync", s_dout, P.total(s_dout))
            _finish(nc, ctx, P)
            return nc
        rms = ln_stats(t, save_stats=True, chunk_ready={k: (s_dve, v) for k, v in stream_ms.items()})
        ln_finish(False, (s_act, rms))
        if t > 0:
            kv_accum(t - 1, heads=range(8, NH))
        ln_cols(t, save_stats=True)
        P.wait("scalar", s_dve, P.total(s_dve))
        P.wait("scalar", s_pool, P.total(s_pool))
        x1_store_ms[t] = P.emit("scalar", lambda e, t=t: e.dma_start(
            out=x1_fm[:, t * NT:(t + 1) * NT].rearrange("(k p) n -> p k n", p=128),
            in_=stream[:].rearrange("p (k n) -> p k n", k=KC)), inc=s_dx1st, amt=16)
        if debug_stop == "ph1":
            P.emit("scalar", lambda e, t=t: e.dma_start(
                out=out_fm[:, t * NT:(t + 1) * NT].rearrange("(k p) n -> p k n", p=128),
                in_=stream[:].rearrange("p (k n) -> p k n", k=KC)), inc=s_dout, amt=16)
        kv_stage(t)
    if ph1_tiles > 0:
        kv_accum(ph1_tiles - 1, heads=range(0, 8))
        kv_accum(ph1_tiles - 1, heads=range(8, NH))

    if debug_stop == "ph1" or ph2_tiles == 0:
        P.wait("sync", s_dout, P.total(s_dout))
        P.wait("sync", s_dx1st, P.total(s_dx1st))
        _finish(nc, ctx, P)
        return nc

    P.emit("vector", lambda e: e.tensor_copy(out=kv_b16[:], in_=kv_acc[:]))
    for h in range(NH):
        kvf = P.emit("vector", lambda e, h=h: e.tensor_scalar(
            ksrep[:, h * 128:(h + 1) * 128], ones128b[:],
            kv_acc[:, h * 129 + 128: h * 129 + 129], None, ALU.mult), inc=s_dve)
    kv_final = kvf

    out_store_ms = [None] * NTILES
    for t in range(ph2_tiles):
        if t == 0:
            P.wait("scalar", s_dx1st, P.total(s_dx1st))
        else:
            P.wait("scalar", s_dout, P.total(s_dout))
        xl_gms = {}
        for g in range(4):
            xl = P.emit("scalar", lambda e, t=t, g=g: e.dma_start(
                out=stream[:].rearrange("p (k n) -> p k n", k=KC)[:, 4 * g:4 * g + 4, :],
                in_=x1_fm[:, t * NT:(t + 1) * NT].rearrange(
                    "(k p) n -> p k n", p=128)[:, 4 * g:4 * g + 4, :]),
                inc=s_dxg[g], amt=16)
            for k in range(4 * g, 4 * g + 4):
                xl_gms[k] = (s_dxg[g], xl)

        P.emit("vector", lambda e, t=t: e.tensor_copy(out=mcols[:], in_=m2save[:, t * TC:(t + 1) * TC]))
        rsv = P.emit("vector", lambda e, t=t: e.tensor_copy(out=rcols[:], in_=r2save[:, t * TC:(t + 1) * TC]),
                     inc=s_dve)
        broadcast_cols((s_dve, rsv))
        bm = P.total(s_pe)
        P.wait("vector", s_pe, bm)
        acq("vector", mean_sb_r)
        P.emit("vector", lambda e: e.tensor_copy(out=mean_sb[:], in_=pbc[0][:]))
        sbc = P.emit("vector", lambda e: e.tensor_copy(out=rstd_sb[:], in_=pbc[1][:]),
                     inc=s_dve)
        normalize(False, bm, sbc, chunk_ready=xl_gms)

        P.wait("vector", s_dve, kv_final)

        def q_block(oc):
            pq, pqr = next_ps()
            slot, lv = ring_load(w_b16["wq_blk"][oc], ("wq_blk",))
            P.wait("tensor", s_ring[slot], lv)
            for k in range(KC):
                P.wait_ms("tensor", lnb_ms[k])
                mm = P.emit("tensor", lambda e, slot=slot, k=k, pq=pq: e.matmul(
                    pq[:], ring[:, slot * SLOT_COLS + k * 128: slot * SLOT_COLS + (k + 1) * 128],
                    lnb[:, k * NT:(k + 1) * NT], start=(k == 0), stop=(k == KC - 1)),
                    inc=(s_pe if k == KC - 1 else None))
            ring_free(slot, mm)
            P.wait("scalar", s_pe, mm)
            sl = oc % 2
            acq("scalar", qtmp_r[sl])
            am = P.emit("scalar", lambda e, pq=pq, sl=sl, oc=oc: e.activation(
                qtmp[:, sl * NT:(sl + 1) * NT], pq[:], AF.Relu, bias=bq_sb[:, oc:oc + 1]),
                inc=s_act)
            P.wait("vector", s_act, am)
            acq("vector", ktm_r)
            qm = P.emit("vector", lambda e, t=t, oc=oc, sl=sl: e.tensor_tensor(
                ktm[:, oc * NT:(oc + 1) * NT], qtmp[:, sl * NT:(sl + 1) * NT],
                decay_bc[:, t * NT:(t + 1) * NT], ALU.mult), inc=s_dve)
            qtmp_r[sl].free = (s_dve, qm)
            pqr.free = (s_dve, qm)
            q_ms[oc] = qm

        def attn_head(h):
            P.wait("tensor", s_dve, q_ms[h])
            pa, par = next_ps()
            pz, pzr = next_ps()
            P.emit("tensor", lambda e, h=h, pa=pa: e.matmul(
                pa[:], kv_b16[:, h * 129: h * 129 + 128], ktm[:, h * NT:(h + 1) * NT],
                start=True, stop=True))
            P.emit("tensor", lambda e, h=h, pz=pz: e.matmul(
                pz[:], ksrep[:, h * 128:(h + 1) * 128], ktm[:, h * NT:(h + 1) * NT],
                start=True, stop=False))
            pe_ms = P.emit("tensor", lambda e, pz=pz: e.matmul(
                pz[:], eps_r1b[:], ones_r512b[:], start=False, stop=True), inc=s_pe)
            P.wait("vector", s_pe, pe_ms)
            sl = h % 2
            P.emit("vector", lambda e, pz=pz, sl=sl: e.reciprocal(
                z_scr[:, sl * NT:(sl + 1) * NT], pz[:]))
            acq("vector", vtm_r)
            av = P.emit("vector", lambda e, h=h, pa=pa, sl=sl: e.tensor_tensor(
                vtm[:, h * NT:(h + 1) * NT], pa[:], z_scr[:, sl * NT:(sl + 1) * NT],
                ALU.mult), inc=s_dve)
            par.free = (s_dve, av)
            pzr.free = (s_dve, av)
            att_ms[h] = av

        for oc in range(KC):
            q_block(oc)
            if oc >= 1:
                attn_head(oc - 1)
        lnb_r.free = (s_pe, P.total(s_pe))
        attn_head(NH - 1)

        for oc in range(KC):
            po, por = next_ps()
            slot, lv = ring_load(w_b16["wo_blk"][oc], ("wo_blk",))
            P.wait("tensor", s_ring[slot], lv)
            for k in range(KC):
                P.wait("tensor", s_dve, att_ms[k])
                mm = P.emit("tensor", lambda e, slot=slot, k=k, po=po: e.matmul(
                    po[:], ring[:, slot * SLOT_COLS + k * 128: slot * SLOT_COLS + (k + 1) * 128],
                    vtm[:, k * NT:(k + 1) * NT], start=(k == 0), stop=(k == KC - 1)),
                    inc=(s_pe if k == KC - 1 else None))
            ring_free(slot, mm)
            P.wait("vector", s_pe, mm)
            sm = P.emit("vector", lambda e, oc=oc, po=po: e.scalar_tensor_tensor(
                stream[:, oc * NT:(oc + 1) * NT], po[:], bo_sb[:, oc:oc + 1],
                stream[:, oc * NT:(oc + 1) * NT], ALU.add, ALU.add), inc=s_dve)
            por.free = (s_dve, sm)
            stream_ms[oc] = sm
        ktm_r.free = (s_pe, P.total(s_pe))
        vtm_r.free = (s_pe, P.total(s_pe))

        rms = ln_stats(t, save_stats=False, chunk_ready={k: (s_dve, v) for k, v in stream_ms.items()})
        ln_finish(True, (s_act, rms))
        gm = router_logits(t)
        ln_cols(t)
        residual_update((s_pe, P.total(s_pe)))
        ffn_stage(t, None, experts=[None])
        comb_last = router_comb(t, gm, nmcols, rcols)
        ffn_stage(t, (s_dve, comb_last), experts=list(range(E)))
        lnb_r.free = (s_pe, P.total(s_pe))

        for g in range(4):
            for oc in range(4 * g, 4 * g + 4):
                P.wait("scalar", s_dve, stream_ms[oc])
            out_store_ms[t] = P.emit("scalar", lambda e, t=t, g=g: e.dma_start(
                out=out_fm[:, t * NT:(t + 1) * NT].rearrange(
                    "(k p) n -> p k n", p=128)[:, 4 * g:4 * g + 4, :],
                in_=stream[:].rearrange("p (k n) -> p k n", k=KC)[:, 4 * g:4 * g + 4, :]),
                inc=s_dout, amt=16)

    P.wait("sync", s_dout, P.total(s_dout))
    _finish(nc, ctx, P)
    return nc


def _finish(nc, ctx, P):
    semh = {name: ctx.enter_context(nc.semaphore(name)) for name in P.sem_names}
    with nc.Block() as block:
        def mk_section(eng_name):
            def section(eng):
                for item in P.ops[eng_name]:
                    if item[0] == "w":
                        eng.wait_ge(semh[item[1]], item[2])
                    else:
                        _, fn, inc, amt = item
                        instr = fn(eng)
                        if inc is not None:
                            instr.then_inc(semh[inc], amt)
            return section
        block.sync(mk_section("sync"))
        block.gpsimd(mk_section("gpsimd"))
        block.vector(mk_section("vector"))
        block.scalar(mk_section("scalar"))
        block.tensor(mk_section("tensor"))
    ctx.close()


def kernel(**inputs):
    w = _prep_inputs(inputs)
    x = np.asarray(inputs["L0_attn_x"], dtype=np.float32)
    nc = build_nc()
    in_maps = []
    for core in range(NCORES):
        m = dict(w)
        m["x_fm"] = np.ascontiguousarray(x[core].T)
        in_maps.append(m)
    res = run_bass_kernel_spmd(nc, in_maps, core_ids=list(range(NCORES)))
    out = np.stack([np.asarray(r["out_fm"]).T for r in res.results])
    return np.ascontiguousarray(out)


if __name__ == "__main__":
    import reference as _ref
    inp = _ref.setup_inputs(0)
    inp = {k: np.asarray(v) for k, v in inp.items()}
    got = kernel(**inp)
    want = np.asarray(_ref.reference(**inp))
    num = float(((got - want) ** 2).mean())
    den = float((want ** 2).mean())
    print("resvar", num / den)


# version 10 (final)
# speedup vs baseline: 1.0601x; 1.0203x over previous
import numpy as np
import ml_dtypes
from contextlib import ExitStack
import concourse.bass as bass
import concourse.mybir as mybir
from concourse.bass_utils import run_bass_kernel_spmd

F32 = mybir.dt.float32
BF16 = mybir.dt.bfloat16
AF = mybir.ActivationFunctionType
ALU = mybir.AluOpType
AX = mybir.AxisListType.X

D = 2048
SEQ = 4096
NT = 512
NTILES = SEQ // NT
KC = D // 128
TC = NT // 128
I = 1408
IC = I // 128
E = 8
NH = 16
LN_EPS = 1e-5
ATT_EPS = 1e-6
NCORES = 8
ENGINES = ("sync", "gpsimd", "vector", "scalar", "tensor")
RING_SLOTS = 4
SLOT_COLS = 2048


class Prog:
    def __init__(self):
        self.ops = {e: [] for e in ENGINES}
        self.cnt = {}
        self.waited = {}
        self.sem_names = []

    def sem(self, name):
        if name not in self.cnt:
            self.cnt[name] = 0
            self.sem_names.append(name)
        return name

    def wait(self, eng, sem, val):
        if val is None or val <= 0:
            return
        assert val <= self.cnt[sem], f"wait on future milestone {sem}={val}>{self.cnt[sem]}"
        key = (eng, sem)
        if self.waited.get(key, 0) >= val:
            return
        self.waited[key] = val
        self.ops[eng].append(("w", sem, val))

    def wait_ms(self, eng, ms):
        if ms is not None:
            self.wait(eng, ms[0], ms[1])

    def emit(self, eng, fn, inc=None, amt=1):
        own = getattr(self, "auto", {}).get(eng)
        if own is not None and (inc is None or inc not in self.dma_sems):
            self.wait(eng, own, self.cnt[own])
            if inc is None:
                inc = own
            assert inc == own, f"{eng} op incs {inc} != own {own}"
        m = None
        if inc is not None:
            self.cnt[inc] += amt
            m = self.cnt[inc]
        self.ops[eng].append(("o", fn, inc, amt))
        return m

    def total(self, sem):
        return self.cnt[sem]


class Res:
    __slots__ = ("free",)
    def __init__(self):
        self.free = None


def _col_blocks(W):
    Dout, Din = W.shape
    C, K = Dout // 128, Din // 128
    WT = np.ascontiguousarray(W.T)
    b = WT.reshape(K, 128, C, 128).transpose(2, 1, 0, 3)
    return np.ascontiguousarray(b.reshape(C, 128, K * 128))


def _slab_pieces(W):
    WT = np.ascontiguousarray(W.T)
    b = WT.reshape(4, 4, 128, 4, 512)
    b = b.transpose(3, 0, 2, 1, 4)
    return np.ascontiguousarray(b.reshape(4, 4, 128, 2048))


def _prep_inputs(inp):
    d = {}
    d["sg_blk"] = _col_blocks(inp["L0_ffn_shared_gate"])
    d["su_blk"] = _col_blocks(inp["L0_ffn_shared_up"])
    d["sd_blk"] = _col_blocks(inp["L0_ffn_shared_down"])
    d["wg_blk"] = np.stack([_col_blocks(inp["L0_ffn_w_gate"][e]) for e in range(E)])
    d["wu_blk"] = np.stack([_col_blocks(inp["L0_ffn_w_up"][e]) for e in range(E)])
    d["wd_blk"] = np.stack([_col_blocks(inp["L0_ffn_w_down"][e]) for e in range(E)])
    d["wk_slab"] = _slab_pieces(inp["L0_attn_Wk"])
    d["wv_slab"] = _slab_pieces(inp["L0_attn_Wv"])
    d["wq_blk"] = _col_blocks(inp["L0_attn_Wq"])
    d["wo_blk"] = _col_blocks(inp["L0_attn_Wo"])
    rwT = np.ascontiguousarray(np.asarray(inp["L0_ffn_router_w"]).T)
    d["rw_f32"] = np.ascontiguousarray(rwT.reshape(16, 128, 8).transpose(1, 0, 2).reshape(128, 128))
    d["bq_pp"] = np.ascontiguousarray(np.asarray(inp["L0_attn_bq"]).reshape(16, 128).T)
    d["bo_pp"] = np.ascontiguousarray(np.asarray(inp["L0_attn_bo"]).reshape(16, 128).T)
    d["bk_row"] = np.ascontiguousarray(np.asarray(inp["L0_attn_bk"]).reshape(1, D))
    d["bv_row"] = np.ascontiguousarray(np.asarray(inp["L0_attn_bv"]).reshape(1, D))
    return {k: np.ascontiguousarray(np.asarray(v, dtype=np.float32)) for k, v in d.items()}


def build_nc(ph1_tiles=NTILES, ph2_tiles=NTILES, debug_stop=None):
    nc = bass.Bass("TRN2", target_bir_lowering=False)
    P = Prog()

    x_fm = nc.dram_tensor("x_fm", [D, SEQ], F32, kind="ExternalInput").ap()
    out_fm = nc.dram_tensor("out_fm", [D, SEQ], F32, kind="ExternalOutput").ap()
    x1_fm = nc.dram_tensor("x1_fm", [D, SEQ], F32, kind="Internal").ap()
    dbg_b = nc.dram_tensor("dbg_b", [128, KC * NT], BF16, kind="ExternalOutput").ap()

    w_in, w_b16 = {}, {}
    def wdecl(name, shape):
        w_in[name] = nc.dram_tensor(name, list(shape), F32, kind="ExternalInput").ap()
        w_b16[name] = nc.dram_tensor(name + "_b", list(shape), BF16, kind="Internal").ap()
    wdecl("sg_blk", (IC, 128, KC * 128))
    wdecl("su_blk", (IC, 128, KC * 128))
    wdecl("sd_blk", (KC, 128, IC * 128))
    wdecl("wg_blk", (E, IC, 128, KC * 128))
    wdecl("wu_blk", (E, IC, 128, KC * 128))
    wdecl("wd_blk", (E, KC, 128, IC * 128))
    wdecl("wk_slab", (4, 4, 128, 2048))
    wdecl("wv_slab", (4, 4, 128, 2048))
    wdecl("wq_blk", (KC, 128, KC * 128))
    wdecl("wo_blk", (KC, 128, KC * 128))
    rw_hbm = nc.dram_tensor("rw_f32", [128, 128], F32, kind="ExternalInput").ap()
    bq_hbm = nc.dram_tensor("bq_pp", [128, KC], F32, kind="ExternalInput").ap()
    bo_hbm = nc.dram_tensor("bo_pp", [128, KC], F32, kind="ExternalInput").ap()
    bk_hbm = nc.dram_tensor("bk_row", [1, D], F32, kind="ExternalInput").ap()
    bv_hbm = nc.dram_tensor("bv_row", [1, D], F32, kind="ExternalInput").ap()

    ctx = ExitStack()
    SB = lambda name, shape, dt: ctx.enter_context(nc.sbuf_tensor(name, shape, dt))
    PSA = lambda name, shape: ctx.enter_context(nc.psum_tensor(name, shape, F32))

    stream = SB("stream", [128, KC * NT], F32)
    lnb = SB("lnb", [128, KC * NT], BF16)
    he = SB("he", [128, IC * NT], BF16)
    sig = SB("sig", [128, 2 * NT], BF16)
    uesc = SB("uesc", [128, 2 * NT], BF16)
    v_scr = SB("v_scr", [128, 2 * NT], F32)
    sq_scr = SB("sq_scr", [128, 2 * NT], BF16)
    qtmp = SB("qtmp", [128, 2 * NT], F32)
    z_scr = SB("z_scr", [128, 2 * NT], F32)
    ktm = SB("ktm", [128, TC * D], BF16)
    vtm = SB("vtm", [128, TC * NH * 129], BF16)
    kv_acc = SB("kv_acc", [128, NH * 129], F32)
    kv_b16 = SB("kv_b16", [128, NH * 129], BF16)
    ksrep = SB("ksrep", [128, NH * 128], BF16)
    comb_tm = SB("comb_tm", [128, TC * 8], BF16)
    combT = SB("combT", [8, NT], BF16)
    comb_bc = SB("comb_bc", [128, E * NT], BF16)
    ring = SB("ring", [128, RING_SLOTS * SLOT_COLS], BF16)
    rw_sb = SB("rw_sb", [128, 128], F32)
    wr_row = SB("wr_row", [1, 8], F32)
    wrsum = SB("wrsum", [128, 8], F32)
    bq_sb = SB("bq_sb", [128, KC], F32)
    bo_sb = SB("bo_sb", [128, KC], F32)
    bk_sb = SB("bk_sb", [1, D], F32)
    bv_sb = SB("bv_sb", [1, D], F32)
    mcols = SB("mcols", [128, TC], F32)
    nmcols = SB("nmcols", [128, TC], F32)
    rcols = SB("rcols", [128, TC], F32)
    m2save = SB("m2save", [128, TC * NTILES], F32)
    r2save = SB("r2save", [128, TC * NTILES], F32)
    rep_sb = SB("rep_sb", [128, 2 * 128], F32)
    lrow = SB("lrow", [128, 8], F32)
    lcol = SB("lcol", [128, 8], F32)
    mask8 = SB("mask8", [128, 8], F32)
    em8 = SB("em8", [128, 8], F32)
    decay_cols = SB("decay_cols", [128, SEQ // 128], F32)
    decay_bc = SB("decay_bc", [128, SEQ], BF16)
    rep_b = SB("rep_b", [128, 128], BF16)
    ones_col = SB("ones_col", [128, 1], F32)
    ones_r1f = SB("ones_r1f", [1, 128], F32)
    ones_r1b = SB("ones_r1b", [1, 128], BF16)
    ones128f = SB("ones128f", [128, 128], F32)
    ones128b = SB("ones128b", [128, 128], BF16)
    ones_r512b = SB("ones_r512b", [1, NT], BF16)
    eps_r1b = SB("eps_r1b", [1, 128], BF16)
    idf32 = SB("idf32", [128, 128], F32)
    idb16 = SB("idb16", [128, 128], BF16)
    iota_t = SB("iota_t", [128, 128], F32)
    pcol = SB("pcol", [128, 1], F32)
    epsln_col = SB("epsln_col", [128, 1], F32)
    one1 = SB("one1", [1, 1], F32)
    ones_col_b = SB("ones_col_b", [128, 1], BF16)
    mrow = SB("mrow", [1, NT], F32)
    mean_sb = SB("mean_sb", [128, NT], F32)
    rstd_sb = SB("rstd_sb", [128, NT], F32)
    wrow = SB("wrow", [1, NT], F32)
    bias_bck = SB("bias_bck", [128, D], BF16)
    bias_bcv = SB("bias_bcv", [128, D], BF16)
    iota8 = SB("iota8", [8, E * 128], BF16)
    sel_all = SB("sel_all", [8, E * 128], BF16)

    ps = [PSA(f"ps{i}", [128, NT]) for i in range(4)]
    pkv = [PSA(f"pkv{i}", [128, NT]) for i in range(2)]
    pbc = [PSA(f"pbc{i}", [128, NT]) for i in range(2)]
    ps_r = [Res() for _ in range(4)]
    pkv_r = [Res() for _ in range(2)]
    pbc_r = [Res() for _ in range(2)]
    sig_r = [Res(), Res()]
    sq_r = [Res(), Res()]
    qtmp_r = [Res(), Res()]
    v_r = [Res(), Res()]
    rep_r = [Res(), Res()]
    ktm_r = Res()
    vtm_r = Res()
    lnb_r = Res()
    mean_sb_r = Res()
    rows_r = Res()

    s_pe = P.sem("pe"); s_dve = P.sem("dve"); s_act = P.sem("act"); s_pool = P.sem("pool")
    s_ring = [P.sem(f"ring{i}") for i in range(RING_SLOTS)]
    s_dx = P.sem("dx")
    s_dx1st = P.sem("dx1st")
    s_dout = P.sem("dout")
    s_dcast = P.sem("dcast")
    s_dxg = [P.sem(f"dxg{g}") for g in range(4)]
    s_wr = [P.sem(f"wr{i}") for i in range(RING_SLOTS)]
    P.auto = {"vector": s_dve, "scalar": s_act, "gpsimd": s_pool}
    P.dma_sems = set([s_dx, s_dx1st, s_dout, s_dcast] + s_ring + s_dxg + s_wr)

    def acq(eng, res):
        if res.free is not None:
            P.wait(eng, res.free[0], res.free[1])

    ring_ns = {"next": 0, "free": [[] for _ in range(RING_SLOTS)]}
    first_seen = set()
    block_ready = {}

    def ring_load(block_id):
        name, idx = block_id
        fp32_ap, b16_ap = w_in[name], w_b16[name]
        for i in idx:
            fp32_ap, b16_ap = fp32_ap[i], b16_ap[i]
        s = ring_ns["next"]
        ring_ns["next"] = (s + 1) % RING_SLOTS
        ncols = b16_ap.shape[-1]
        slot_ap = ring[:, s * SLOT_COLS: s * SLOT_COLS + ncols]
        if block_id not in first_seen:
            first_seen.add(block_id)
            for ms in ring_ns["free"][s]:
                P.wait_ms("gpsimd", ms)
            lv = P.emit("gpsimd", lambda e, o=slot_ap, i=fp32_ap: e.dma_start(out=o, in_=i),
                        inc=s_ring[s], amt=16)
            P.wait("sync", s_ring[s], lv)
            wv = P.emit("sync", lambda e, o=b16_ap, i=slot_ap: e.dma_start(out=o, in_=i),
                        inc=s_wr[s], amt=16)
            block_ready[block_id] = (s_wr[s], wv)
            ring_ns["free"][s] = [(s_wr[s], wv)]
        else:
            P.wait_ms("sync", block_ready[block_id])
            for ms in ring_ns["free"][s]:
                P.wait_ms("sync", ms)
            lv = P.emit("sync", lambda e, o=slot_ap, i=b16_ap: e.dma_start(out=o, in_=i),
                        inc=s_ring[s], amt=16)
            ring_ns["free"][s] = []
        return s, lv

    def ring_free(slot, pe_ms):
        ring_ns["free"][slot].append((s_pe, pe_ms))

    for dst, src in ((rw_sb, rw_hbm), (bq_sb, bq_hbm), (bo_sb, bo_hbm),
                     (bk_sb, bk_hbm), (bv_sb, bv_hbm)):
        P.emit("scalar", lambda e, d=dst, s=src: e.dma_start(out=d[:], in_=s),
               inc=s_dx, amt=16)
    consts_ld = P.total(s_dx)
    for eng in ("tensor", "vector", "scalar"):
        P.wait(eng, s_dx, consts_ld)

    P.emit("gpsimd", lambda e: e.iota(iota_t[:], [[1, 128]], channel_multiplier=0,
                                      allow_small_or_imprecise_dtypes=True))
    P.emit("gpsimd", lambda e: e.iota(pcol[:], [[0, 1]], channel_multiplier=1,
                                      allow_small_or_imprecise_dtypes=True))
    P.emit("gpsimd", lambda e: e.iota(decay_cols[:], [[128, SEQ // 128]], channel_multiplier=1,
                                      allow_small_or_imprecise_dtypes=True))
    P.emit("gpsimd", lambda e: e.memset(ones_col[:], 1.0))
    P.emit("gpsimd", lambda e: e.memset(ones_r1f[:], 1.0))
    P.emit("gpsimd", lambda e: e.memset(ones_r1b[:], 1.0))
    P.emit("gpsimd", lambda e: e.memset(ones128f[:], 1.0))
    P.emit("gpsimd", lambda e: e.memset(ones128b[:], 1.0))
    P.emit("gpsimd", lambda e: e.memset(ones_r512b[:], 1.0))
    P.emit("gpsimd", lambda e: e.memset(eps_r1b[:], ATT_EPS))
    P.emit("gpsimd", lambda e: e.memset(kv_acc[:], 0.0))
    P.emit("gpsimd", lambda e: e.memset(epsln_col[:], LN_EPS))
    P.emit("gpsimd", lambda e: e.memset(one1[:], 1.0))
    P.emit("gpsimd", lambda e: e.memset(ones_col_b[:], 1.0))
    P.emit("gpsimd", lambda e: e.iota(iota8[:], [[1, E], [0, 128]], channel_multiplier=0,
                                      allow_small_or_imprecise_dtypes=True))
    for c in range(TC):
        for h in range(NH):
            off = c * NH * 129 + h * 129 + 128
            P.emit("gpsimd", lambda e, off=off: e.memset(vtm[:, off:off + 1], 1.0))
    pool_init = P.emit("gpsimd", lambda e: e.memset(em8[:], 0.0), inc=s_pool)

    P.wait("vector", s_pool, pool_init)
    P.emit("vector", lambda e: e.tensor_scalar(idf32[:], iota_t[:], pcol[:], None, ALU.is_equal))
    P.emit("vector", lambda e: e.tensor_scalar(idb16[:], iota_t[:], pcol[:], None, ALU.is_equal))
    id_ms = P.emit("vector", lambda e: e.tensor_scalar(sel_all[:], iota8[:], pcol[0:8, :], None, ALU.is_equal),
                   inc=s_dve)
    P.wait("scalar", s_pool, pool_init)
    dec_ms = P.emit("scalar", lambda e: e.activation(decay_cols[:], decay_cols[:], AF.Exp,
                                                     scale=-1.0 / SEQ), inc=s_act)
    P.wait("vector", s_act, dec_ms)
    P.wait("tensor", s_dve, id_ms)
    rep_b_r = Res()
    for gc in range(SEQ // 128):
        pb = pbc[gc % 2]
        acq("vector", rep_b_r)
        dv = P.emit("vector", lambda e, gc=gc: e.tensor_scalar(
            rep_b[:], ones128b[:], decay_cols[:, gc:gc + 1], None, ALU.mult), inc=s_dve)
        P.wait("tensor", s_dve, dv)
        acq("tensor", pbc_r[gc % 2])
        pm = P.emit("tensor", lambda e, pb=pb: e.matmul(
            pb[:, 0:128], rep_b[:], idb16[:], start=True, stop=True), inc=s_pe)
        rep_b_r.free = (s_pe, pm)
        P.wait("vector", s_pe, pm)
        dvc = P.emit("vector", lambda e, gc=gc, pb=pb: e.tensor_copy(
            out=decay_bc[:, gc * 128:(gc + 1) * 128], in_=pb[:, 0:128]), inc=s_dve)
        (pbc_r[gc % 2]).free = (s_dve, dvc)

    P.wait("tensor", s_dx, consts_ld)
    acq("tensor", pkv_r[0])
    for k in range(KC):
        mmi = P.emit("tensor", lambda e, k=k: e.matmul(
            pkv[0][0:1, 0:8], ones_col[:], rw_sb[:, k * 8:(k + 1) * 8],
            start=(k == 0), stop=(k == KC - 1)), inc=(s_pe if k == KC - 1 else None))
    P.wait("vector", s_pe, mmi)
    wrr_ms = P.emit("vector", lambda e: e.tensor_copy(out=wr_row[:], in_=pkv[0][0:1, 0:8]),
                    inc=s_dve)
    pkv_r[0].free = (s_dve, wrr_ms)
    P.wait("tensor", s_dve, wrr_ms)
    acq("tensor", pkv_r[1])
    pm = P.emit("tensor", lambda e: e.matmul(pkv[1][:, 0:8], ones_r1f[:], wr_row[:],
                                             start=True, stop=True), inc=s_pe)
    P.wait("vector", s_pe, pm)
    wrs_ms = P.emit("vector", lambda e: e.tensor_copy(out=wrsum[:], in_=pkv[1][:, 0:8]),
                    inc=s_dve)
    pkv_r[1].free = (s_dve, wrs_ms)

    for brow_sb, bdst in ((bk_sb, bias_bck), (bv_sb, bias_bcv)):
        for seg in range(4):
            pb = pbc[seg % 2]
            acq("tensor", pbc_r[seg % 2])
            pm = P.emit("tensor", lambda e, pb=pb, brow_sb=brow_sb, seg=seg: e.matmul(
                pb[:], ones_r1f[:], brow_sb[:, seg * NT:(seg + 1) * NT],
                start=True, stop=True), inc=s_pe)
            P.wait("vector", s_pe, pm)
            dv = P.emit("vector", lambda e, pb=pb, bdst=bdst, seg=seg: e.tensor_copy(
                out=bdst[:, seg * NT:(seg + 1) * NT], in_=pb[:]), inc=s_dve)
            pbc_r[seg % 2].free = (s_dve, dv)


    rot = {"i": 0}
    lnb_ms = {}
    he_ms = {}
    stream_ms = {}
    kv_ev = {}
    q_ms = {}
    att_ms = {}

    def next_ps(eng="tensor"):
        i = rot["i"] % 4
        rot["i"] += 1
        acq(eng, ps_r[i])
        return ps[i], ps_r[i]

    def wait_lnb(eng):
        for k in range(KC):
            P.wait_ms(eng, lnb_ms[k])

    def ln_stats(t, save_stats, need_cols=True, chunk_ready=None):
        pstat, pstat_r = ps[0], ps_r[0]
        pstatq, pstatq_r = ps[1], ps_r[1]
        acq("tensor", pstat_r)
        acq("tensor", pstatq_r)
        acq("gpsimd", lnb_r)
        for k in range(KC):
            sl = k % 2
            if chunk_ready is not None:
                P.wait_ms("gpsimd", chunk_ready[k])
                P.wait_ms("scalar", chunk_ready[k])
            if k in lnb_ms:
                P.wait_ms("gpsimd", lnb_ms[k])
            xb_ms = P.emit("gpsimd", lambda e, k=k: e.tensor_copy(
                out=lnb[:, k * NT:(k + 1) * NT], in_=stream[:, k * NT:(k + 1) * NT]))
            acq("scalar", sq_r[sl])
            sq_ms = P.emit("scalar", lambda e, k=k, sl=sl: e.activation(
                sq_scr[:, sl * NT:(sl + 1) * NT], stream[:, k * NT:(k + 1) * NT], AF.Square),
                inc=s_act)
            P.wait("tensor", s_pool, xb_ms)
            P.wait("tensor", s_act, sq_ms)
            P.emit("tensor", lambda e, k=k: e.matmul(
                pstat[0:1, 0:NT], ones_col_b[:], lnb[:, k * NT:(k + 1) * NT],
                start=(k == 0), stop=(k == KC - 1)))
            pe_ms = P.emit("tensor", lambda e, k=k, sl=sl: e.matmul(
                pstatq[0:1, 0:NT], ones_col_b[:], sq_scr[:, sl * NT:(sl + 1) * NT],
                start=(k == 0), stop=(k == KC - 1)), inc=s_pe)
            sq_r[sl].free = (s_pe, pe_ms)
        stats_pe = pe_ms
        P.wait("vector", s_pe, stats_pe)
        acq("vector", rows_r)
        P.emit("vector", lambda e: e.tensor_scalar(mrow[:], pstat[0:1, 0:NT], 1.0 / D, None, ALU.mult))
        wr_ms0 = P.emit("vector", lambda e: e.tensor_scalar(wrow[:], pstatq[0:1, 0:NT], 1.0 / D, None, ALU.mult),
                        inc=s_dve)
        P.wait("scalar", s_dve, wr_ms0)
        m2_ms = P.emit("scalar", lambda e: e.activation(z_scr[0:1, 0:NT], mrow[:], AF.Square),
                       inc=s_act)
        P.wait("vector", s_act, m2_ms)
        vr_ms = P.emit("vector", lambda e: e.tensor_tensor(wrow[:], wrow[:], z_scr[0:1, 0:NT], ALU.subtract),
                       inc=s_dve)
        P.wait("scalar", s_dve, vr_ms)
        P.emit("scalar", lambda e: e.activation(wrow[:], wrow[:], AF.Ln, bias=epsln_col[0:1, :]))
        rstd_ms = P.emit("scalar", lambda e: e.activation(wrow[:], wrow[:], AF.Exp, scale=-0.5),
                         inc=s_act)
        pstatq_r.free = (s_dve, wr_ms0)
        _ln_state["vr_ms"] = vr_ms
        _ln_state["rstd_ms"] = rstd_ms
        _ln_state["pstat"] = pstat
        _ln_state["pstat_r"] = pstat_r
        return rstd_ms

    _ln_state = {}

    def ln_cols(t, save_stats=False):
        pstat, pstat_r = _ln_state["pstat"], _ln_state["pstat_r"]
        P.wait("tensor", s_dve, _ln_state["vr_ms"])
        P.wait("tensor", s_act, _ln_state["rstd_ms"])
        for c in range(TC):
            P.emit("tensor", lambda e, c=c, pstat=pstat: e.transpose(
                pstat[:, 64 + 4 * c: 65 + 4 * c], mrow[0:1, c * 128:(c + 1) * 128], one1[:]))
            pe_ms = P.emit("tensor", lambda e, c=c, pstat=pstat: e.transpose(
                pstat[:, 80 + 4 * c: 81 + 4 * c], wrow[0:1, c * 128:(c + 1) * 128], one1[:]),
                inc=(s_pe if c == TC - 1 else None))
        P.wait("vector", s_pe, pe_ms)
        for c in range(TC):
            P.emit("vector", lambda e, c=c, pstat=pstat: e.tensor_copy(
                out=mcols[:, c:c + 1], in_=pstat[:, 64 + 4 * c: 65 + 4 * c]))
            P.emit("vector", lambda e, c=c, pstat=pstat: e.tensor_copy(
                out=rcols[:, c:c + 1], in_=pstat[:, 80 + 4 * c: 81 + 4 * c]))
        nm_ms = P.emit("vector", lambda e: e.tensor_scalar(nmcols[:], mcols[:], -1.0, None, ALU.mult),
                       inc=s_dve)
        pstat_r.free = (s_dve, nm_ms)
        rows_r.free = (s_pe, pe_ms)
        if save_stats:
            P.emit("vector", lambda e, t=t: e.tensor_copy(out=m2save[:, t * TC:(t + 1) * TC], in_=mcols[:]))
            P.emit("vector", lambda e, t=t: e.tensor_copy(out=r2save[:, t * TC:(t + 1) * TC], in_=rcols[:]))

    def ln_finish(with_residual, ready_ms, chunk_ready=None):
        P.wait_ms("tensor", ready_ms)
        acq("tensor", pbc_r[0])
        P.emit("tensor", lambda e: e.matmul(pbc[0][:], ones_r1f[:], mrow[:],
                                            start=True, stop=True))
        acq("tensor", pbc_r[1])
        bc_ms = P.emit("tensor", lambda e: e.matmul(pbc[1][:], ones_r1f[:], wrow[:],
                                                    start=True, stop=True), inc=s_pe)
        P.wait("vector", s_pe, bc_ms)
        acq("vector", mean_sb_r)
        P.emit("vector", lambda e: e.tensor_copy(out=mean_sb[:], in_=pbc[0][:]))
        sbc_ms = P.emit("vector", lambda e: e.tensor_copy(out=rstd_sb[:], in_=pbc[1][:]),
                        inc=s_dve)
        return normalize(with_residual, bc_ms, sbc_ms, chunk_ready=chunk_ready)

    def broadcast_cols(ready_ms):
        P.wait_ms("vector", ready_ms)
        for c in range(TC):
            sl = c % 2
            acq("vector", rep_r[sl])
            dv = P.emit("vector", lambda e, c=c, sl=sl: e.tensor_scalar(
                rep_sb[:, sl * 128:(sl + 1) * 128], ones128f[:], mcols[:, c:c + 1], None, ALU.mult),
                inc=s_dve)
            P.wait("tensor", s_dve, dv)
            if c == 0:
                acq("tensor", pbc_r[0])
            pm = P.emit("tensor", lambda e, c=c, sl=sl: e.matmul(
                pbc[0][:, c * 128:(c + 1) * 128], rep_sb[:, sl * 128:(sl + 1) * 128], idf32[:],
                start=True, stop=True), inc=s_pe)
            rep_r[sl].free = (s_pe, pm)
            acq("vector", rep_r[sl])
            dv = P.emit("vector", lambda e, c=c, sl=sl: e.tensor_scalar(
                rep_sb[:, sl * 128:(sl + 1) * 128], ones128f[:], rcols[:, c:c + 1], None, ALU.mult),
                inc=s_dve)
            P.wait("tensor", s_dve, dv)
            if c == 0:
                acq("tensor", pbc_r[1])
            pm = P.emit("tensor", lambda e, c=c, sl=sl: e.matmul(
                pbc[1][:, c * 128:(c + 1) * 128], rep_sb[:, sl * 128:(sl + 1) * 128], idf32[:],
                start=True, stop=True), inc=s_pe)
            rep_r[sl].free = (s_pe, pm)
        return pm

    def normalize(with_residual, bc_ms, sbc_ms, chunk_ready=None):
        P.wait("vector", s_pe, bc_ms)
        P.wait("gpsimd", s_pe, bc_ms)
        last_dve = None
        last_pool = None
        for k in range(KC):
            on_dve = (k % 2 == 0)
            sl = 0 if on_dve else 1
            eng = "vector" if on_dve else "gpsimd"
            if chunk_ready is not None:
                P.wait_ms(eng, chunk_ready[k])
            if on_dve:
                P.emit("vector", lambda e, k=k, sl=sl: e.tensor_tensor(
                    v_scr[:, sl * NT:(sl + 1) * NT], stream[:, k * NT:(k + 1) * NT],
                    pbc[0][:], ALU.subtract))
                lm = P.emit("vector", lambda e, k=k, sl=sl: e.tensor_tensor(
                    lnb[:, k * NT:(k + 1) * NT], v_scr[:, sl * NT:(sl + 1) * NT],
                    pbc[1][:], ALU.mult), inc=s_dve)
                last_dve = lm
                lnb_ms[k] = (s_dve, lm)
            else:
                P.wait_ms("gpsimd", (s_dve, sbc_ms))
                P.emit("gpsimd", lambda e, k=k, sl=sl: e.tensor_tensor(
                    v_scr[:, sl * NT:(sl + 1) * NT], stream[:, k * NT:(k + 1) * NT],
                    mean_sb[:], ALU.subtract))
                lm = P.emit("gpsimd", lambda e, k=k, sl=sl: e.tensor_tensor(
                    lnb[:, k * NT:(k + 1) * NT], v_scr[:, sl * NT:(sl + 1) * NT],
                    rstd_sb[:], ALU.mult), inc=s_pool)
                last_pool = lm
                lnb_ms[k] = (s_pool, lm)
        pbc_r[0].free = (s_dve, last_dve)
        pbc_r[1].free = (s_dve, last_dve)
        if last_pool is not None:
            mean_sb_r.free = (s_pool, last_pool)
        return last_dve

    def residual_update(gate_ms=None):
        last = None
        for k in range(KC):
            P.wait_ms("vector", lnb_ms[k])
            if gate_ms is not None:
                P.wait_ms("vector", gate_ms)
            last = P.emit("vector", lambda e, k=k: e.scalar_tensor_tensor(
                stream[:, k * NT:(k + 1) * NT], lnb[:, k * NT:(k + 1) * NT], 0.5,
                stream[:, k * NT:(k + 1) * NT], ALU.mult, ALU.add), inc=s_dve)
        return last

    def router_logits(t):
        acq("tensor", pkv_r[1])
        gm = None
        for c in range(TC):
            for k in range(KC):
                gm = P.emit("tensor", lambda e, c=c, k=k: e.matmul(
                    pkv[1][:, c * 8:(c + 1) * 8],
                    stream[:, k * NT + c * 128: k * NT + (c + 1) * 128],
                    rw_sb[:, k * 8:(k + 1) * 8],
                    start=(k == 0), stop=(k == KC - 1)),
                    inc=(s_pe if (k == KC - 1 and c == TC - 1) else None))
        return gm

    def router_comb(t, gm, nmc, rc):
        ctv = None
        for c in range(TC):
            P.wait("vector", s_pe, gm)
            P.wait("vector", s_dve, wrs_ms)
            P.emit("vector", lambda e, c=c, nmc=nmc: e.scalar_tensor_tensor(
                lrow[:], wrsum[:], nmc[:, c:c + 1], pkv[1][:, c * 8:(c + 1) * 8],
                ALU.mult, ALU.add))
            P.emit("vector", lambda e: e.tensor_reduce(
                lcol[:, 0:1], lrow[:], axis=AX, op=ALU.max, negate=True))
            hm = P.emit("vector", lambda e, c=c, rc=rc: e.tensor_scalar(
                lrow[:], lrow[:], lcol[:, 0:1], rc[:, c:c + 1], ALU.add, ALU.mult),
                inc=s_dve)
            P.wait("scalar", s_dve, hm)
            em = P.emit("scalar", lambda e: e.activation(lrow[:], lrow[:], AF.Exp), inc=s_act)
            P.wait("vector", s_act, em)
            P.emit("vector", lambda e: e.tensor_reduce(lcol[:, 1:2], lrow[:], axis=AX, op=ALU.add))
            P.emit("vector", lambda e: e.reciprocal(lcol[:, 2:3], lcol[:, 1:2]))
            P.emit("vector", lambda e: e.tensor_reduce(lcol[:, 3:4], lrow[:], axis=AX, op=ALU.max))
            P.emit("vector", lambda e: e.tensor_scalar(mask8[:], lrow[:], lcol[:, 3:4], None, ALU.is_ge))
            P.emit("vector", lambda e: e.scalar_tensor_tensor(em8[:], mask8[:], -1e30, lrow[:],
                                                              ALU.mult, ALU.add))
            P.emit("vector", lambda e: e.tensor_reduce(lcol[:, 5:6], em8[:], axis=AX, op=ALU.max))
            P.emit("vector", lambda e: e.tensor_scalar(mask8[:], lrow[:], lcol[:, 5:6], None, ALU.is_ge))
            cm = P.emit("vector", lambda e, c=c: e.scalar_tensor_tensor(
                comb_tm[:, c * 8:(c + 1) * 8], mask8[:], lcol[:, 2:3], lrow[:],
                ALU.mult, ALU.mult), inc=s_dve)
            if c == TC - 1:
                pkv_r[1].free = (s_dve, cm)
            P.wait("tensor", s_dve, cm)
            if c == 0:
                acq("tensor", pkv_r[0])
            pm = P.emit("tensor", lambda e, c=c: e.matmul(
                pkv[0][0:8, c * 128:(c + 1) * 128], comb_tm[:, c * 8:(c + 1) * 8], idb16[:],
                start=True, stop=True), inc=s_pe)
            P.wait("vector", s_pe, pm)
            ctv = P.emit("vector", lambda e, c=c: e.tensor_copy(
                out=combT[:, c * 128:(c + 1) * 128], in_=pkv[0][0:8, c * 128:(c + 1) * 128]),
                inc=s_dve)
        pkv_r[0].free = (s_dve, ctv)
        P.wait("tensor", s_dve, ctv)
        last = None
        for ee in range(E):
            pb, pr = pbc[ee % 2], pbc_r[ee % 2]
            acq("tensor", pr)
            pm = P.emit("tensor", lambda e, ee=ee, pb=pb: e.matmul(
                pb[:], sel_all[:, ee * 128:(ee + 1) * 128], combT[:], start=True, stop=True), inc=s_pe)
            P.wait("vector", s_pe, pm)
            last = P.emit("vector", lambda e, ee=ee, pb=pb: e.tensor_copy(
                out=comb_bc[:, ee * NT:(ee + 1) * NT], in_=pb[:]), inc=s_dve)
            pr.free = (s_dve, last)
        return last

    def ffn_stage(t, comb_ready, experts=None):
        last_sm = None
        if experts is None:
            experts = [None] + list(range(E))
        for ex in experts:
            is_shared = ex is None
            for c in range(IC):
                gid = ("sg_blk", (c,)) if is_shared else ("wg_blk", (ex, c))
                uid = ("su_blk", (c,)) if is_shared else ("wu_blk", (ex, c))
                pg, pgr = next_ps()
                pu, pur = next_ps()
                for bid, pp in ((gid, pg), (uid, pu)):
                    slot, lv = ring_load(bid)
                    P.wait("tensor", s_ring[slot], lv)
                    for k in range(KC):
                        P.wait_ms("tensor", lnb_ms[k])
                        mm = P.emit("tensor", lambda e, slot=slot, k=k, pp=pp: e.matmul(
                            pp[:], ring[:, slot * SLOT_COLS + k * 128: slot * SLOT_COLS + (k + 1) * 128],
                            lnb[:, k * NT:(k + 1) * NT], start=(k == 0), stop=(k == KC - 1)),
                            inc=(s_pe if k == KC - 1 else None))
                    ring_free(slot, mm)
                g_pe, u_pe = None, mm
                P.wait("scalar", s_pe, u_pe)
                sl = c % 2
                acq("scalar", sig_r[sl])
                am = P.emit("scalar", lambda e, pg=pg, sl=sl: e.activation(
                    sig[:, sl * NT:(sl + 1) * NT], pg[:], AF.Sigmoid), inc=s_act)
                P.wait("vector", s_act, am)
                if is_shared:
                    hm = P.emit("vector", lambda e, c=c, pu=pu, sl=sl: e.tensor_tensor(
                        he[:, c * NT:(c + 1) * NT], sig[:, sl * NT:(sl + 1) * NT], pu[:],
                        ALU.mult), inc=s_dve)
                else:
                    P.wait_ms("vector", comb_ready)
                    P.emit("vector", lambda e, pu=pu, sl=sl, ex=ex: e.tensor_tensor(
                        uesc[:, sl * NT:(sl + 1) * NT], pu[:],
                        comb_bc[:, ex * NT:(ex + 1) * NT], ALU.mult))
                    hm = P.emit("vector", lambda e, c=c, sl=sl: e.tensor_tensor(
                        he[:, c * NT:(c + 1) * NT], sig[:, sl * NT:(sl + 1) * NT],
                        uesc[:, sl * NT:(sl + 1) * NT], ALU.mult), inc=s_dve)
                sig_r[sl].free = (s_dve, hm)
                pgr.free = (s_dve, hm)
                pur.free = (s_dve, hm)
                he_ms[c] = hm
            for oc in range(KC):
                did = ("sd_blk", (oc,)) if is_shared else ("wd_blk", (ex, oc))
                pd, pdr = next_ps()
                slot, lv = ring_load(did)
                P.wait("tensor", s_ring[slot], lv)
                for ic in range(IC):
                    P.wait("tensor", s_dve, he_ms[ic])
                    mm = P.emit("tensor", lambda e, slot=slot, ic=ic, pd=pd: e.matmul(
                        pd[:], ring[:, slot * SLOT_COLS + ic * 128: slot * SLOT_COLS + (ic + 1) * 128],
                        he[:, ic * NT:(ic + 1) * NT], start=(ic == 0), stop=(ic == IC - 1)),
                        inc=(s_pe if ic == IC - 1 else None))
                ring_free(slot, mm)
                P.wait("vector", s_pe, mm)
                sm = P.emit("vector", lambda e, oc=oc, pd=pd: e.scalar_tensor_tensor(
                    stream[:, oc * NT:(oc + 1) * NT], pd[:], 0.5,
                    stream[:, oc * NT:(oc + 1) * NT], ALU.mult, ALU.add), inc=s_dve)
                pdr.free = (s_dve, sm)
                stream_ms[oc] = sm
                last_sm = sm
        return last_sm

    def kv_stage(t):
        for mat, bias_row in (("wk_slab", bk_sb), ("wv_slab", bv_sb)):
            for dseg in range(4):
                for c in range(TC):
                    acq("tensor", ps_r[c])
                for kg in range(4):
                    slot, lv = ring_load((mat, (dseg, kg)))
                    P.wait("tensor", s_ring[slot], lv)
                    for c in range(TC):
                        for kk in range(4):
                            k = kg * 4 + kk
                            P.wait_ms("tensor", lnb_ms[k])
                            mm = P.emit("tensor", lambda e, slot=slot, kk=kk, k=k, c=c, kg=kg: e.matmul(
                                ps[c][:], lnb[:, k * NT + c * 128: k * NT + (c + 1) * 128],
                                ring[:, slot * SLOT_COLS + kk * 512: slot * SLOT_COLS + (kk + 1) * 512],
                                start=(kg == 0 and kk == 0), stop=(kg == 3 and kk == 3)),
                                inc=(s_pe if (c == TC - 1 and kk == 3) else None))
                    ring_free(slot, mm)
                for c in range(TC):
                    pe_ms = mm if c == TC - 1 else None
                    P.wait("vector" if mat == "wv_slab" else "vector", s_pe, mm)
                    if mat == "wk_slab":
                        bb = bias_bck
                        sl = c % 2
                        acq("vector", qtmp_r[sl])
                        dv = P.emit("vector", lambda e, c=c, dseg=dseg, sl=sl, bb=bb: e.tensor_tensor(
                            qtmp[:, sl * NT:(sl + 1) * NT], ps[c][:],
                            bb[:, dseg * 512:(dseg + 1) * 512], ALU.add), inc=s_dve)
                        ps_r[c].free = (s_dve, dv)
                        P.wait("scalar", s_dve, dv)
                        if t == 0 and dseg == 0 and c == 0:
                            acq("scalar", ktm_r)
                        ev = P.emit("scalar", lambda e, t=t, c=c, dseg=dseg, sl=sl: e.activation(
                            ktm[:, c * D + dseg * 512: c * D + (dseg + 1) * 512],
                            qtmp[:, sl * NT:(sl + 1) * NT],
                            AF.Relu, scale=decay_cols[:, t * TC + c: t * TC + c + 1]), inc=s_act)
                        qtmp_r[sl].free = (s_act, ev)
                        kv_ev[(mat, dseg, c)] = (s_act, ev)
                    else:
                        if t == 0 and dseg == 0 and c == 0:
                            acq("vector", vtm_r)
                        base = c * NH * 129 + dseg * 4 * 129
                        dst_ap = vtm[:, base: base + 4 * 129].rearrange(
                            "p (h d) -> p h d", h=4)[:, :, 0:128]
                        ev = P.emit("vector", lambda e, c=c, dseg=dseg, dst_ap=dst_ap: e.tensor_tensor(
                            out=dst_ap, in0=ps[c][:].rearrange("p (h d) -> p h d", h=4),
                            in1=bias_bcv[:, dseg * 512:(dseg + 1) * 512].rearrange("p (h d) -> p h d", h=4),
                            op=ALU.add),
                            inc=s_dve)
                        kv_ev[(mat, dseg, c)] = (s_dve, ev)
                        ps_r[c].free = (s_dve, ev)
        lnb_r.free = (s_pe, P.total(s_pe))
        return None

    def kv_accum(t, heads=range(NH)):
        last_dv = None
        for h in heads:
            dseg = h // 4
            for c in range(TC):
                P.wait_ms("tensor", kv_ev[("wk_slab", dseg, c)])
                P.wait_ms("tensor", kv_ev[("wv_slab", dseg, c)])
            pp, pr = pkv[h % 2], pkv_r[h % 2]
            acq("tensor", pr)
            for c in range(TC):
                mm = P.emit("tensor", lambda e, h=h, c=c, pp=pp: e.matmul(
                    pp[:, 0:129], ktm[:, c * D + h * 128: c * D + (h + 1) * 128],
                    vtm[:, c * NH * 129 + h * 129: c * NH * 129 + (h + 1) * 129],
                    start=(c == 0), stop=(c == TC - 1)),
                    inc=(s_pe if c == TC - 1 else None))
            P.wait("vector", s_pe, mm)
            last_dv = P.emit("vector", lambda e, h=h, pp=pp: e.tensor_tensor(
                kv_acc[:, h * 129:(h + 1) * 129], pp[:, 0:129],
                kv_acc[:, h * 129:(h + 1) * 129], ALU.add), inc=s_dve)
            pr.free = (s_dve, last_dv)
        if list(heads)[-1] == NH - 1:
            ktm_r.free = (s_pe, P.total(s_pe))
            vtm_r.free = (s_pe, P.total(s_pe))
            lnb_r.free = (s_pe, P.total(s_pe))
        return last_dv

    x1_store_ms = [None] * NTILES
    for t in range(ph1_tiles):
        if t > 0:
            P.wait("scalar", s_dx1st, x1_store_ms[t - 1])
        xl_gms = {}
        for g in range(4):
            xl = P.emit("scalar", lambda e, t=t, g=g: e.dma_start(
                out=stream[:].rearrange("p (k n) -> p k n", k=KC)[:, 4 * g:4 * g + 4, :],
                in_=x_fm[:, t * NT:(t + 1) * NT].rearrange(
                    "(k p) n -> p k n", p=128)[:, 4 * g:4 * g + 4, :]),
                inc=s_dxg[g], amt=16)
            for k in range(4 * g, 4 * g + 4):
                xl_gms[k] = (s_dxg[g], xl)
        rms = ln_stats(t, save_stats=False, chunk_ready=xl_gms)
        if debug_stop == "stats":
            P.wait("gpsimd", s_act, rms)
            P.emit("gpsimd", lambda e: e.dma_start(out=dbg_b[:, 0:TC], in_=mcols[:]), inc=s_dout, amt=16)
            P.emit("gpsimd", lambda e: e.dma_start(out=dbg_b[:, TC:2 * TC], in_=rcols[:]), inc=s_dout, amt=16)
            P.emit("gpsimd", lambda e: e.dma_start(out=dbg_b[:, 2 * TC:5 * TC], in_=vcols[:]), inc=s_dout, amt=16)
            P.emit("gpsimd", lambda e: e.dma_start(out=dbg_b[:, 5 * TC:6 * TC], in_=nmcols[:]), inc=s_dout, amt=16)
            P.wait("sync", s_dout, P.total(s_dout))
            _finish(nc, ctx, P)
            return nc
        ln_finish(True, (s_act, rms))
        if t > 0:
            kv_accum(t - 1, heads=range(0, 8))
        gm = router_logits(t)
        ln_cols(t)
        residual_update((s_pe, P.total(s_pe)))
        if debug_stop in ("ln1", "router"):
            if debug_stop == "router":
                router_comb(t, gm, nmcols, rcols)
            P.wait("scalar", s_dve, P.total(s_dve))
            P.wait("scalar", s_pool, P.total(s_pool))
            src = lnb if debug_stop == "ln1" else comb_bc
            P.emit("scalar", lambda e, src=src: e.dma_start(
                out=dbg_b[:, 0:src.shape[-1]], in_=src[:]), inc=s_dout, amt=16)
            P.emit("scalar", lambda e, t=t: e.dma_start(
                out=out_fm[:, t * NT:(t + 1) * NT].rearrange("(k p) n -> p k n", p=128),
                in_=stream[:].rearrange("p (k n) -> p k n", k=KC)), inc=s_dout, amt=16)
            P.wait("sync", s_dout, P.total(s_dout))
            _finish(nc, ctx, P)
            return nc
        ffn_stage(t, None, experts=[None])
        if debug_stop != "ffnsh":
            comb_last = router_comb(t, gm, nmcols, rcols)
            ffn_stage(t, (s_dve, comb_last), experts=list(range(E)))
        lnb_r.free = (s_pe, P.total(s_pe))
        if debug_stop in ("ffnsh", "ffn1"):
            for oc in range(KC):
                P.wait("scalar", s_dve, stream_ms[oc])
            P.wait("scalar", s_dve, P.total(s_dve))
            P.emit("scalar", lambda e, t=t: e.dma_start(
                out=out_fm[:, t * NT:(t + 1) * NT].rearrange("(k p) n -> p k n", p=128),
                in_=stream[:].rearrange("p (k n) -> p k n", k=KC)), inc=s_dout, amt=16)
            P.wait("sync", s_dout, P.total(s_dout))
            _finish(nc, ctx, P)
            return nc
        rms = ln_stats(t, save_stats=True, chunk_ready={k: (s_dve, v) for k, v in stream_ms.items()})
        ln_finish(False, (s_act, rms))
        if t > 0:
            kv_accum(t - 1, heads=range(8, NH))
        ln_cols(t, save_stats=True)
        P.wait("scalar", s_dve, P.total(s_dve))
        P.wait("scalar", s_pool, P.total(s_pool))
        x1_store_ms[t] = P.emit("scalar", lambda e, t=t: e.dma_start(
            out=x1_fm[:, t * NT:(t + 1) * NT].rearrange("(k p) n -> p k n", p=128),
            in_=stream[:].rearrange("p (k n) -> p k n", k=KC)), inc=s_dx1st, amt=16)
        if debug_stop == "ph1":
            P.emit("scalar", lambda e, t=t: e.dma_start(
                out=out_fm[:, t * NT:(t + 1) * NT].rearrange("(k p) n -> p k n", p=128),
                in_=stream[:].rearrange("p (k n) -> p k n", k=KC)), inc=s_dout, amt=16)
        kv_stage(t)
    if ph1_tiles > 0:
        kv_accum(ph1_tiles - 1, heads=range(0, 8))
        kv_accum(ph1_tiles - 1, heads=range(8, NH))

    if debug_stop == "ph1" or ph2_tiles == 0:
        P.wait("sync", s_dout, P.total(s_dout))
        P.wait("sync", s_dx1st, P.total(s_dx1st))
        _finish(nc, ctx, P)
        return nc

    P.emit("vector", lambda e: e.tensor_copy(out=kv_b16[:], in_=kv_acc[:]))
    for h in range(NH):
        kvf = P.emit("vector", lambda e, h=h: e.tensor_scalar(
            ksrep[:, h * 128:(h + 1) * 128], ones128b[:],
            kv_acc[:, h * 129 + 128: h * 129 + 129], None, ALU.mult), inc=s_dve)
    kv_final = kvf

    out_store_ms = [None] * NTILES
    for t in range(ph2_tiles):
        if t == 0:
            P.wait("scalar", s_dx1st, P.total(s_dx1st))
        else:
            P.wait("scalar", s_dout, P.total(s_dout))
        xl_gms = {}
        for g in range(4):
            xl = P.emit("scalar", lambda e, t=t, g=g: e.dma_start(
                out=stream[:].rearrange("p (k n) -> p k n", k=KC)[:, 4 * g:4 * g + 4, :],
                in_=x1_fm[:, t * NT:(t + 1) * NT].rearrange(
                    "(k p) n -> p k n", p=128)[:, 4 * g:4 * g + 4, :]),
                inc=s_dxg[g], amt=16)
            for k in range(4 * g, 4 * g + 4):
                xl_gms[k] = (s_dxg[g], xl)

        P.emit("vector", lambda e, t=t: e.tensor_copy(out=mcols[:], in_=m2save[:, t * TC:(t + 1) * TC]))
        rsv = P.emit("vector", lambda e, t=t: e.tensor_copy(out=rcols[:], in_=r2save[:, t * TC:(t + 1) * TC]),
                     inc=s_dve)
        broadcast_cols((s_dve, rsv))
        bm = P.total(s_pe)
        P.wait("vector", s_pe, bm)
        acq("vector", mean_sb_r)
        P.emit("vector", lambda e: e.tensor_copy(out=mean_sb[:], in_=pbc[0][:]))
        sbc = P.emit("vector", lambda e: e.tensor_copy(out=rstd_sb[:], in_=pbc[1][:]),
                     inc=s_dve)
        normalize(False, bm, sbc, chunk_ready=xl_gms)

        P.wait("vector", s_dve, kv_final)

        def q_block(oc):
            pq, pqr = next_ps()
            slot, lv = ring_load(("wq_blk", (oc,)))
            P.wait("tensor", s_ring[slot], lv)
            for k in range(KC):
                P.wait_ms("tensor", lnb_ms[k])
                mm = P.emit("tensor", lambda e, slot=slot, k=k, pq=pq: e.matmul(
                    pq[:], ring[:, slot * SLOT_COLS + k * 128: slot * SLOT_COLS + (k + 1) * 128],
                    lnb[:, k * NT:(k + 1) * NT], start=(k == 0), stop=(k == KC - 1)),
                    inc=(s_pe if k == KC - 1 else None))
            ring_free(slot, mm)
            P.wait("scalar", s_pe, mm)
            sl = oc % 2
            acq("scalar", qtmp_r[sl])
            am = P.emit("scalar", lambda e, pq=pq, sl=sl, oc=oc: e.activation(
                qtmp[:, sl * NT:(sl + 1) * NT], pq[:], AF.Relu, bias=bq_sb[:, oc:oc + 1]),
                inc=s_act)
            P.wait("vector", s_act, am)
            acq("vector", ktm_r)
            qm = P.emit("vector", lambda e, t=t, oc=oc, sl=sl: e.tensor_tensor(
                ktm[:, oc * NT:(oc + 1) * NT], qtmp[:, sl * NT:(sl + 1) * NT],
                decay_bc[:, t * NT:(t + 1) * NT], ALU.mult), inc=s_dve)
            qtmp_r[sl].free = (s_dve, qm)
            pqr.free = (s_dve, qm)
            q_ms[oc] = qm

        def attn_head(h):
            P.wait("tensor", s_dve, q_ms[h])
            pa, par = next_ps()
            pz, pzr = next_ps()
            P.emit("tensor", lambda e, h=h, pa=pa: e.matmul(
                pa[:], kv_b16[:, h * 129: h * 129 + 128], ktm[:, h * NT:(h + 1) * NT],
                start=True, stop=True))
            P.emit("tensor", lambda e, h=h, pz=pz: e.matmul(
                pz[:], ksrep[:, h * 128:(h + 1) * 128], ktm[:, h * NT:(h + 1) * NT],
                start=True, stop=False))
            pe_ms = P.emit("tensor", lambda e, pz=pz: e.matmul(
                pz[:], eps_r1b[:], ones_r512b[:], start=False, stop=True), inc=s_pe)
            P.wait("vector", s_pe, pe_ms)
            sl = h % 2
            P.emit("vector", lambda e, pz=pz, sl=sl: e.reciprocal(
                z_scr[:, sl * NT:(sl + 1) * NT], pz[:]))
            acq("vector", vtm_r)
            av = P.emit("vector", lambda e, h=h, pa=pa, sl=sl: e.tensor_tensor(
                vtm[:, h * NT:(h + 1) * NT], pa[:], z_scr[:, sl * NT:(sl + 1) * NT],
                ALU.mult), inc=s_dve)
            par.free = (s_dve, av)
            pzr.free = (s_dve, av)
            att_ms[h] = av

        for oc in range(KC):
            q_block(oc)
            if oc >= 1:
                attn_head(oc - 1)
        lnb_r.free = (s_pe, P.total(s_pe))
        attn_head(NH - 1)

        for oc in range(KC):
            po, por = next_ps()
            slot, lv = ring_load(("wo_blk", (oc,)))
            P.wait("tensor", s_ring[slot], lv)
            for k in range(KC):
                P.wait("tensor", s_dve, att_ms[k])
                mm = P.emit("tensor", lambda e, slot=slot, k=k, po=po: e.matmul(
                    po[:], ring[:, slot * SLOT_COLS + k * 128: slot * SLOT_COLS + (k + 1) * 128],
                    vtm[:, k * NT:(k + 1) * NT], start=(k == 0), stop=(k == KC - 1)),
                    inc=(s_pe if k == KC - 1 else None))
            ring_free(slot, mm)
            P.wait("vector", s_pe, mm)
            sm = P.emit("vector", lambda e, oc=oc, po=po: e.scalar_tensor_tensor(
                stream[:, oc * NT:(oc + 1) * NT], po[:], bo_sb[:, oc:oc + 1],
                stream[:, oc * NT:(oc + 1) * NT], ALU.add, ALU.add), inc=s_dve)
            por.free = (s_dve, sm)
            stream_ms[oc] = sm
        ktm_r.free = (s_pe, P.total(s_pe))
        vtm_r.free = (s_pe, P.total(s_pe))

        rms = ln_stats(t, save_stats=False, chunk_ready={k: (s_dve, v) for k, v in stream_ms.items()})
        ln_finish(True, (s_act, rms))
        gm = router_logits(t)
        ln_cols(t)
        residual_update((s_pe, P.total(s_pe)))
        ffn_stage(t, None, experts=[None])
        comb_last = router_comb(t, gm, nmcols, rcols)
        ffn_stage(t, (s_dve, comb_last), experts=list(range(E)))
        lnb_r.free = (s_pe, P.total(s_pe))

        for g in range(4):
            for oc in range(4 * g, 4 * g + 4):
                P.wait("scalar", s_dve, stream_ms[oc])
            out_store_ms[t] = P.emit("scalar", lambda e, t=t, g=g: e.dma_start(
                out=out_fm[:, t * NT:(t + 1) * NT].rearrange(
                    "(k p) n -> p k n", p=128)[:, 4 * g:4 * g + 4, :],
                in_=stream[:].rearrange("p (k n) -> p k n", k=KC)[:, 4 * g:4 * g + 4, :]),
                inc=s_dout, amt=16)

    P.wait("sync", s_dout, P.total(s_dout))
    _finish(nc, ctx, P)
    return nc


def _finish(nc, ctx, P):
    semh = {name: ctx.enter_context(nc.semaphore(name)) for name in P.sem_names}
    with nc.Block() as block:
        def mk_section(eng_name):
            def section(eng):
                for item in P.ops[eng_name]:
                    if item[0] == "w":
                        eng.wait_ge(semh[item[1]], item[2])
                    else:
                        _, fn, inc, amt = item
                        instr = fn(eng)
                        if inc is not None:
                            instr.then_inc(semh[inc], amt)
            return section
        block.sync(mk_section("sync"))
        block.gpsimd(mk_section("gpsimd"))
        block.vector(mk_section("vector"))
        block.scalar(mk_section("scalar"))
        block.tensor(mk_section("tensor"))
    ctx.close()


def kernel(**inputs):
    w = _prep_inputs(inputs)
    x = np.asarray(inputs["L0_attn_x"], dtype=np.float32)
    nc = build_nc()
    in_maps = []
    for core in range(NCORES):
        m = dict(w)
        m["x_fm"] = np.ascontiguousarray(x[core].T)
        in_maps.append(m)
    res = run_bass_kernel_spmd(nc, in_maps, core_ids=list(range(NCORES)))
    out = np.stack([np.asarray(r["out_fm"]).T for r in res.results])
    return np.ascontiguousarray(out)


if __name__ == "__main__":
    import reference as _ref
    inp = _ref.setup_inputs(0)
    inp = {k: np.asarray(v) for k, v in inp.items()}
    got = kernel(**inp)
    want = np.asarray(_ref.reference(**inp))
    num = float(((got - want) ** 2).mean())
    den = float((want ** 2).mean())
    print("resvar", num / den)
